# version 12
# baseline (speedup 1.0000x reference)
import contextlib
import numpy as np
import concourse.bass as bass
import concourse.mybir as mybir
from concourse.bass_utils import run_bass_kernel_spmd

F32 = mybir.dt.float32
BF16 = mybir.dt.bfloat16
AF = mybir.ActivationFunctionType
OP = mybir.AluOpType
AXX = mybir.AxisListType.X


class CFG:
    def __init__(self, T=2048, D=1024, H=16, F=1024, CMSH=4096, TT=512,
                 split=False, debug=False):
        self.T, self.D, self.H, self.F, self.CMSH, self.TT = T, D, H, F, CMSH, TT
        self.KC = D // 128
        self.HD = 64
        self.CH = T // 128
        self.HJ = F // 128
        self.HJ2 = self.HJ // 2
        self.NE = 9
        self.CMSHC = CMSH // 128
        self.LEVELS = 3
        self.EPS = 1e-5
        self.NT = T // TT
        self.OCF = D // 128
        self.HHD = H * self.HD
        self.NSB = (H + 7) // 8
        self.split = split
        self.debug = debug
        self.barlvl = 0


class Sem:
    __slots__ = ("h", "n", "name")

    def __init__(self, h, name):
        self.h, self.n, self.name = h, 0, name


class Prog:
    ENGS = ("tensor", "vector", "scalar", "sync", "gpsimd")

    def __init__(self, nc):
        self.nc = nc
        self.q = {e: [] for e in self.ENGS}
        self.bank_guard = [None] * 8
        self.self_sem = {}
        self.dbg_guard = None
        self.all_sems = []

    def sem(self, stack, name):
        s = Sem(stack.enter_context(self.nc.semaphore(name)), name)
        self.all_sems.append(s)
        return s

    def barrier(self, exclude=()):
        fence = [(s, s.n) for s in self.all_sems if s.n > 0 and s not in exclude]
        for e in self.ENGS:
            self.on(e, None, w=fence)

    def bar(self, lvl):
        if getattr(self, "barlvl", 0) >= lvl:
            self.barrier()

    def on(self, eng, fn, *args, w=(), i=(), **kw):
        i = list(i)
        w = list(w)
        if self.dbg_guard is not None:
            w.append(self.dbg_guard)
        waits = []
        ss = self.self_sem.get(eng)
        if ss is not None and fn is not None and fn != "dma_start":
            if ss.n > 0:
                waits.append((ss.h, ss.n))
            if not any(s is ss for s, _ in i):
                i.append((ss, 1))
        for x in w:
            if x is None:
                continue
            s, v = x
            assert v <= s.n, f"wait on future {s.name}: {v} > {s.n}"
            if v > 0 and (s is not ss or fn is None or fn == "dma_start"):
                waits.append((s.h, v))
        incs = []
        for s, amt in i:
            s.n += amt
            incs.append((s.h, amt))
        self.q[eng].append((fn, args, kw, waits, incs))

    def wait_only(self, eng, w):
        self.on(eng, None, w=w)

    def dma(self, queue, out, in_, sem, w=()):
        self.on(queue, "dma_start", w=w, i=[(sem, 16)], out=out, in_=in_)

    def bank_w(self, *banks):
        ws = []
        for b in banks:
            g = self.bank_guard[b]
            if g is not None:
                ws.append(g)
                self.bank_guard[b] = None
        return ws

    def bank_done(self, bank, sem):
        self.bank_guard[bank] = (sem, sem.n)

    def replay(self):
        nc = self.nc
        with nc.Block() as block:
            def make(eng_name):
                items = self.q[eng_name]

                def body(eng):
                    for (fn, args, kw, waits, incs) in items:
                        for h, v in waits:
                            eng.wait_ge(h, v)
                        if fn is None:
                            continue
                        ins = getattr(eng, fn)(*args, **kw)
                        for h, amt in incs:
                            ins.then_inc(h, amt)
                return body
            block.tensor(make("tensor"))
            block.vector(make("vector"))
            block.scalar(make("scalar"))
            block.sync(make("sync"))
            block.gpsimd(make("gpsimd"))


class TileStream:

    def __init__(self, p, queue, slots, stack, name):
        self.p, self.queue, self.slots = p, queue, slots
        self.sems = [p.sem(stack, f"{name}_{i}") for i in range(len(slots))]
        self.uses = [0] * len(slots)
        self.guard = [None] * len(slots)
        self.nxt = 0
        self.pend = []

    def request(self, src_ap, extra_w=()):
        s = self.nxt
        self.nxt = (self.nxt + 1) % len(self.slots)
        w = ([self.guard[s]] if self.guard[s] else []) + list(extra_w)
        self.p.dma(self.queue, self.slots[s], src_ap, self.sems[s], w=w)
        self.uses[s] += 1
        self.guard[s] = None
        self.pend.append((s, (self.sems[s], 16 * self.uses[s])))

    def take(self):
        return self.pend.pop(0)

    def done(self, s, sem):
        self.guard[s] = (sem, sem.n)


class SplitPump:

    def __init__(self, p, stack, stage_ap, sDV, sAC, queue="sync"):
        self.p, self.stage, self.sDV, self.queue = p, stage_ap, sDV, queue
        self.sAC = sAC
        self.sem = p.sem(stack, "dspl")
        self.uses = 0
        self.guard = None

    def feed(self, src_f32, hi_ap, lo_ap, extra_w=()):
        p = self.p
        n = 1
        for st, ct in src_f32.ap[1:]:
            n *= ct
        stage = self.stage[:, 0:n]
        w = ([self.guard] if self.guard else []) + list(extra_w)
        p.dma(self.queue, stage, src_f32, self.sem, w=w)
        self.uses += 1
        rdy = (self.sem, 16 * self.uses)
        p.on("scalar", "copy", hi_ap, stage, w=[rdy], i=[(self.sAC, 1)])
        p.on("vector", "tensor_tensor", lo_ap, stage, hi_ap, OP.subtract,
             w=[(self.sAC, self.sAC.n)], i=[(self.sDV, 1)])
        self.guard = (self.sDV, self.sDV.n)
        return self.guard


class SplitTileStream:

    def __init__(self, p, pump, slots):
        self.p, self.pump, self.slots = p, pump, slots
        self.guard = [None] * len(slots)
        self.nxt = 0
        self.pend = []

    def request(self, src_ap, extra_w=()):
        s = self.nxt
        self.nxt = (self.nxt + 1) % len(self.slots)
        g = ([self.guard[s]] if self.guard[s] else []) + list(extra_w)
        rdy = self.pump.feed(src_ap, self.slots[s][0], self.slots[s][1],
                             extra_w=g)
        self.guard[s] = None
        self.pend.append((s, rdy))

    def take(self):
        return self.pend.pop(0)

    def done(self, s, sem):
        self.guard[s] = (sem, sem.n)


class SpillLane:

    def __init__(self, p, queue, stack, name, n):
        self.p, self.queue = p, queue
        self.sems = [p.sem(stack, f"{name}_{i}") for i in range(n)]
        self.uses = [0] * n

    def write_wait(self, s):
        return (self.sems[s], 16 * self.uses[s]) if self.uses[s] else None

    def spill(self, s, dst, src, w=()):
        self.p.dma(self.queue, dst, src, self.sems[s], w=w)
        self.uses[s] += 1

    def total(self, s):
        return (self.sems[s], 16 * self.uses[s])



def stage_inputs(inputs, cfg):
    c = cfg
    f32 = np.float32
    cc = lambda a: np.ascontiguousarray(a, dtype=f32)
    x = np.asarray(inputs["L0_attn_x"])
    KC, HJ, D = c.KC, c.HJ, c.D

    w1cat = np.concatenate([np.asarray(inputs["L0_ffn_rw1"]),
                            np.asarray(inputs["L0_ffn_sw1"])], 0)
    w2cat = np.concatenate([np.asarray(inputs["L0_ffn_rw2"]),
                            np.asarray(inputs["L0_ffn_sw2"])], 0)
    b1cat = np.concatenate([np.asarray(inputs["L0_ffn_rb1"]),
                            np.asarray(inputs["L0_ffn_sb1"])], 0)
    b2cat = np.concatenate([np.asarray(inputs["L0_ffn_rb2"]),
                            np.asarray(inputs["L0_ffn_sb2"])], 0)

    w1v = w1cat.reshape(9, KC, 128, 2 * c.F)
    a_p = w1v[:, :, :, :c.F].reshape(9, KC, 128, HJ, 128)
    b_p = w1v[:, :, :, c.F:].reshape(9, KC, 128, HJ, 128)
    w1s = cc(np.concatenate([a_p, b_p], 3).transpose(0, 3, 2, 1, 4))
    w2s = cc(w2cat.reshape(9, HJ, 128, D))
    b1s = cc(b1cat.reshape(9, 2, HJ, 128).transpose(3, 0, 1, 2).reshape(128, 9 * 2 * HJ))
    b2stack = cc(b2cat)

    gw = cc(np.asarray(inputs["L0_ffn_gate_w"]).reshape(KC, 128, 8)
             .transpose(1, 0, 2).reshape(128, KC * 8))
    gb = cc(np.asarray(inputs["L0_ffn_gate_b"]).reshape(8, 1))

    def stat_stage(w):
        M = w.shape[1]
        return cc(w.reshape(KC, 128, M // 128, 128).transpose(2, 1, 0, 3))

    wqs = stat_stage(np.asarray(inputs["L0_attn_wq"]))
    wks = stat_stage(np.asarray(inputs["L0_attn_wk"]))
    wos = stat_stage(np.asarray(inputs["L0_attn_wo"]))
    wvm = cc(np.asarray(inputs["L0_attn_wv"]).reshape(KC, 128, c.HHD))

    lnv = np.stack([np.asarray(inputs["L0_attn_ln1_g"]),
                    np.asarray(inputs["L0_attn_ln1_b"]),
                    np.asarray(inputs["L0_attn_ln2_g"]),
                    np.asarray(inputs["L0_attn_ln2_b"])], 0)
    lnv = cc(lnv.reshape(4, KC, 128).transpose(2, 0, 1).reshape(128, 4 * KC))

    CMSHC = c.CMSHC
    cw1s = cc(np.asarray(inputs["L0_attn_cms_w1"])
              .reshape(3, KC, 128, CMSHC, 128).transpose(0, 3, 2, 1, 4))
    cw2s = cc(np.asarray(inputs["L0_attn_cms_w2"]).reshape(3, CMSHC, 128, D))
    cb1s = cc(np.asarray(inputs["L0_attn_cms_b1"])
              .reshape(3, CMSHC, 128).transpose(2, 0, 1).reshape(128, 3 * CMSHC))
    cb2s = cc(np.asarray(inputs["L0_attn_cms_b2"])
              .reshape(3, KC, 128).transpose(2, 0, 1).reshape(128, 3 * KC))

    ident = np.eye(128, dtype=f32)
    sel9 = np.zeros((9, 8 * 128), dtype=f32)
    for e_ in range(8):
        sel9[e_, e_ * 128:(e_ + 1) * 128] = 1.0
    tril = (np.arange(128)[:, None] <= np.arange(128)[None, :]).astype(f32)

    common = dict(w1s=w1s, w2s=w2s, b1s=b1s, b2stack=b2stack, gw=gw, gb=gb,
                  wqs=wqs, wks=wks, wos=wos, wvm=wvm, lnv=lnv, cw1s=cw1s,
                  cw2s=cw2s, cb1s=cb1s, cb2s=cb2s, ident=ident, tril=tril,
                  sel9=sel9)
    in_maps = []
    for core in range(8):
        b, seg = core // 4, core % 4
        m = dict(common)
        m["x"] = cc(x[b, seg * c.T:(seg + 1) * c.T, :].T)
        pm = np.zeros((64, 4), dtype=f32)
        pm[:, :seg] = 1.0
        m["pmask"] = pm
        in_maps.append(m)
    return in_maps



def build(cfg):
    c = cfg
    nc = bass.Bass("TRN2", target_bir_lowering=False,
                   dynamic_dma_scratch_size=2048)
    T, D, KC, HJ, CMSHC = c.T, c.D, c.KC, c.HJ, c.CMSHC
    dram = lambda n, s, k="ExternalInput": nc.dram_tensor(n, s, F32, kind=k).ap()
    io = {
        "x": dram("x", [D, T]),
        "w1s": dram("w1s", [9, 2 * HJ, 128, KC, 128]),
        "w2s": dram("w2s", [9, HJ, 128, D]),
        "b1s": dram("b1s", [128, 9 * 2 * HJ]),
        "b2stack": dram("b2stack", [9, D]),
        "gw": dram("gw", [128, KC * 8]),
        "gb": dram("gb", [8, 1]),
        "wqs": dram("wqs", [KC, 128, KC, 128]),
        "wks": dram("wks", [KC, 128, KC, 128]),
        "wos": dram("wos", [KC, 128, KC, 128]),
        "wvm": dram("wvm", [KC, 128, c.HHD]),
        "lnv": dram("lnv", [128, 4 * KC]),
        "cw1s": dram("cw1s", [3, CMSHC, 128, KC, 128]),
        "cw2s": dram("cw2s", [3, CMSHC, 128, D]),
        "cb1s": dram("cb1s", [128, 3 * CMSHC]),
        "cb2s": dram("cb2s", [128, 3 * KC]),
        "ident": dram("ident", [128, 128]),
        "sel9": dram("sel9", [9, 8 * 128]),
        "tril": dram("tril", [128, 128]),
        "pmask": dram("pmask", [64, 4]),
        "out": dram("out", [D, T], "ExternalOutput"),
    }
    scr = lambda n, s: nc.dram_tensor(n, s, F32, kind="Internal").ap()
    io["scr_x1"] = scr("scr_x1", [128, KC, T])
    io["scr_ain"] = scr("scr_ain", [128, KC, T])
    io["scr_acc"] = scr("scr_acc", [128, KC, T])
    io["scr_q"] = scr("scr_q", [KC, 128, T])
    io["scr_kfm"] = scr("scr_kfm", [KC, 128, T])
    io["cc_in"] = scr("cc_in", [64, c.HHD])
    io["cc_out"] = scr("cc_out", [4, 64, c.HHD])
    if c.debug:
        for nm in ("d_xhat", "d_x1", "d_h1", "d_x2p", "d_x2"):
            io[nm] = dram(nm, [128, KC, T], "ExternalOutput")
        io["d_cw"] = dram("d_cw", [9, T], "ExternalOutput")
        io["d_ssum"] = dram("d_ssum", [64, c.HHD], "ExternalOutput")
        io["d_mstart"] = dram("d_mstart", [64, c.HHD], "ExternalOutput")
        io["d_y"] = dram("d_y", [128, KC, T], "ExternalOutput")
        io["d_u16a"] = dram("d_u16a", [128, T], "ExternalOutput")
        io["d_u16b"] = dram("d_u16b", [128, T], "ExternalOutput")
        io["d_ain"] = dram("d_ain", [128, KC, T], "ExternalOutput")

    p = Prog(nc)
    p.barlvl = getattr(cfg, "barlvl", 0)
    with contextlib.ExitStack() as stack:
        _body(p, c, io, stack)
        p.replay()
    return nc


def _body(p, c, io, stack):
    nc = p.nc
    T, D, KC, NT, TT, CH, HJ, HJ2 = c.T, c.D, c.KC, c.NT, c.TT, c.CH, c.HJ, c.HJ2
    sb = lambda n, s, dt=F32: stack.enter_context(nc.sbuf_tensor(n, s, dt))

    A = sb("A", [128, KC, T])
    Bt = sb("B", [128, KC, T])
    Av = A[:, :, :].rearrange("p k t -> p (k t)").rearrange(
        "p (ch m) -> p ch m", m=c.HHD)
    psum = stack.enter_context(nc.psum_tensor("ps", [128, 8, 512], F32))
    ones = sb("ones", [128, 128])
    ident = sb("s_ident", [128, 128])
    lnv = sb("s_lnv", [128, 4 * KC])
    cb1s = sb("s_cb1s", [128, 3 * c.CMSHC])
    cb2s = sb("s_cb2s", [128, 3 * KC])
    epst = sb("s_eps", [128, 1])
    U16a = sb("U16a", [128, T])
    U16b = sb("U16b", [128, T])
    sqst = sb("sqst", [128, T])
    if c.split:
        wsplit = sb("wsplit", [128, 1024])

        def halves(t):
            fb = t[:, :, :].rearrange("p k t -> p (k t)").bitcast(BF16)
            fb = fb.rearrange("p (s k t) -> p s k t", s=2, t=T)
            return fb[:, 0], fb[:, 1]
        Ah, Al = halves(A)
        Bh, Bl = halves(Bt)
        U16ab = U16a[:, :].bitcast(BF16)
        U16bb = U16b[:, :].bitcast(BF16)
        hilo = {id(A): (Ah, Al), id(Bt): (Bh, Bl)}

    sPE = p.sem(stack, "sPE")
    sDV = p.sem(stack, "sDV")
    sAC = p.sem(stack, "sAC")
    dC = p.sem(stack, "dC")
    dX = p.sem(stack, "dX")
    sCC = p.sem(stack, "sCC")
    dDBG = p.sem(stack, "dDBG")
    dVM = p.sem(stack, "dVM")
    dCC = p.sem(stack, "dCC")
    dX1 = p.sem(stack, "dX1")
    dRA = p.sem(stack, "dRA")
    dRX = p.sem(stack, "dRX")
    dACC = p.sem(stack, "dACC")
    dRC = p.sem(stack, "dRC")
    dOUT = p.sem(stack, "dOUT")
    dY = p.sem(stack, "dY")
    p.self_sem = {"vector": sDV, "scalar": sAC}
    pump = SplitPump(p, stack, wsplit[:, :], sDV, sAC) if c.split else None

    def mm3(ps_ap, stat, mov, start, stop, w=(), i=()):
        if not c.split or not isinstance(stat, tuple):
            p.on("tensor", "matmul", ps_ap, stat, mov, start=start, stop=stop,
                 w=w, i=i)
            return
        sh, sl_ = stat
        mh, ml = mov
        p.on("tensor", "matmul", ps_ap, sh, mh, start=start, stop=False, w=w)
        p.on("tensor", "matmul", ps_ap, sh, ml, start=False, stop=False)
        p.on("tensor", "matmul", ps_ap, sl_, mh, start=False, stop=stop, i=i)

    def split2(hi_ap, lo_ap, src_ap, w=()):
        p.on("scalar", "copy", hi_ap, src_ap,
             w=list(w) + [(sDV, sDV.n)], i=[(sAC, 1)])
        p.on("vector", "tensor_tensor", lo_ap, src_ap, hi_ap, OP.subtract,
             w=[(sAC, sAC.n)], i=[(sDV, 1)])

    xv = io["x"].rearrange("(k p) t -> p k t", p=128)
    dXf = [p.sem(stack, f"dXc{f}") for f in range(KC)]
    _xq = ["scalar", "sync", "gpsimd"]
    for f in range(KC):
        p.dma(_xq[f % 3], A[:, f, :], xv[:, f, :], dXf[f])
    x_ready = (dXf[KC - 1], 16)
    x_chunk = [(dXf[f], 16) for f in range(KC)]
    p.dma("sync", lnv[:, :], io["lnv"][:, :], dC)
    p.dma("sync", cb1s[:, :], io["cb1s"][:, :], dC)
    p.dma("sync", cb2s[:, :], io["cb2s"][:, :], dC)
    p.dma("sync", ident[:, :], io["ident"][:, :], dC)
    consts = (dC, dC.n)
    p.on("vector", "memset", ones[:, :], 1.0, i=[(sDV, 1)])
    p.on("vector", "memset", epst[:, :], c.EPS, i=[(sDV, 1)])
    init_done = (sDV, sDV.n)

    def dump(name, ap, extra=()):
        if c.debug:
            p.dbg_guard = None
            p.dma("gpsimd", io[name][:, :] if len(io[name].shape) == 2
                  else io[name][:, :, :], ap, dDBG,
                  w=[(sDV, sDV.n), (sAC, sAC.n), (sPE, sPE.n)] + list(extra))
            p.dbg_guard = (dDBG, dDBG.n)

    def ln_stats(src_f, ready, extra_c1=False, ready_f=None):
        for f in range(KC):
            ap = src_f(f)
            rdy_f = ready_f[f] if ready_f is not None else ready
            wsq = [rdy_f, (sPE, sPE.n)] if ready_f is not None else \
                ([ready, (sPE, sPE.n)] if f == 0 else [(sPE, sPE.n)])
            p.on("scalar", "activation", sqst[:, :], ap, AF.Square,
                 w=wsq, i=[(sAC, 1)])
            for tt in range(NT):
                wts = ([rdy_f, init_done] + p.bank_w(tt)) if f == 0 else \
                    ([rdy_f] if ready_f is not None else [])
                p.on("tensor", "matmul", psum[:, tt, :TT], ones[:, :],
                     ap[:, tt * TT:(tt + 1) * TT], start=(f == 0),
                     stop=(f == KC - 1), w=wts)
            for tt in range(NT):
                wts = [(sAC, sAC.n)]
                if f == 0:
                    wts += p.bank_w(NT + tt)
                p.on("tensor", "matmul", psum[:, NT + tt, :TT], ones[:, :],
                     sqst[:, tt * TT:(tt + 1) * TT], start=(f == 0),
                     stop=(f == KC - 1), w=wts,
                     i=[(sPE, 1)] if tt == NT - 1 else ())
        mm_done = (sPE, sPE.n)
        inv_n = 1.0 / D
        for tt in range(NT):
            sl = slice(tt * TT, (tt + 1) * TT)
            p.on("vector", "tensor_scalar_mul", sqst[:, sl], psum[:, tt, :TT],
                 inv_n, w=[mm_done], i=[(sDV, 1)])
            p.on("vector", "tensor_scalar_mul", U16a[:, sl],
                 psum[:, NT + tt, :TT], inv_n, i=[(sDV, 1)])
            p.on("vector", "scalar_tensor_tensor", U16b[:, sl],
                 sqst[:, sl], -1.0, sqst[:, sl], OP.mult, OP.mult, i=[(sDV, 1)])
            p.on("vector", "tensor_tensor", U16a[:, sl],
                 U16a[:, sl], U16b[:, sl], OP.add, i=[(sDV, 1)])
            p.on("scalar", "activation", U16a[:, sl], U16a[:, sl],
                 AF.Sqrt, bias=epst[:, 0:1], w=[(sDV, sDV.n)], i=[(sAC, 1)])
            p.on("vector", "reciprocal", U16a[:, sl], U16a[:, sl],
                 w=[(sAC, sAC.n)], i=[(sDV, 1)])
            p.on("vector", "scalar_tensor_tensor", U16b[:, sl], sqst[:, sl],
                 -1.0, U16a[:, sl], OP.mult, OP.mult, i=[(sDV, 1)])
            p.bank_done(tt, sDV)
            p.bank_done(NT + tt, sDV)

    def emit_ffn(res, xhat, ready, tag=[0], ready_f=None, out_cb=None):
        tg_ = tag[0]; tag[0] += 1
        nm = lambda s: f"{s}{tg_}"
        ln_stats(lambda f: res[:, f, :], ready, ready_f=ready_f)
        st = (sDV, sDV.n)
        if c.split:
            xh_h, xh_l = hilo[id(xhat)]
        xhat_f = []
        for f in range(KC):
            p.on("vector", "tensor_tensor", sqst[:, :], res[:, f, :],
                 U16a[:, :], OP.mult, w=[st] if f == 0 else [],
                 i=[(sDV, 1)])
            if not c.split:
                p.on("vector", "tensor_tensor", xhat[:, f, :], sqst[:, :],
                     U16b[:, :], OP.add, i=[(sDV, 1)])
            else:
                p.on("vector", "tensor_tensor", sqst[:, :], sqst[:, :],
                     U16b[:, :], OP.add, i=[(sDV, 1)])
                split2(xh_h[:, f, :], xh_l[:, f, :], sqst[:, :])
            xhat_f.append((sDV, sDV.n))
        xhat_done = (sDV, sDV.n)
        p.bar(1)

        lgp = sqst
        ffc = p.sem(stack, nm("dffc"))
        with nc.sbuf_tensor(nm("b1s"), [128, 9 * 2 * HJ], F32) as b1s, \
                nc.sbuf_tensor(nm("gwt"), [128, KC * 8], F32) as gwt, \
                nc.sbuf_tensor(nm("gwt_h"), [128, KC * 8], BF16) as gwt_h, \
                nc.sbuf_tensor(nm("gwt_l"), [128, KC * 8], BF16) as gwt_l, \
                nc.sbuf_tensor(nm("gbt"), [8, 1], F32) as gbt, \
                nc.sbuf_tensor(nm("sel9t"), [9, 8 * 128], F32) as sel9t, \
                nc.sbuf_tensor(nm("cwfm"), [9, T], F32) as cwfm, \
                nc.sbuf_tensor(nm("gtmp"), [128, CH, 8], F32) as gtmp, \
                nc.sbuf_tensor(nm("cwtok"), [128, CH, 8], F32) as cwtok, \
                nc.sbuf_tensor(nm("grow"), [128, 2], F32) as grow:
            dSEL = p.sem(stack, nm("dsel"))
            p.dma("sync", sel9t[:, :], io["sel9"][:, :], dSEL)
            sel_r = (dSEL, 16)
            p.dma("sync", b1s[:, :], io["b1s"][:, :], ffc,
                  w=[(sPE, sPE.n), (sAC, sAC.n)])
            p.dma("sync", gwt[:, :], io["gw"][:, :], ffc)
            p.dma("sync", gbt[:, :], io["gb"][:, :], ffc)
            ffr = (ffc, 48)
            gw_r = ffr
            if c.split:
                split2(gwt_h[:, :], gwt_l[:, :], gwt[:, :], w=[ffr])
                gw_r = (sDV, sDV.n)
            for tt in range(NT):
                for k in range(KC):
                    wts = [xhat_f[k]]
                    if k == 0:
                        wts = p.bank_w(tt) + [xhat_f[0], gw_r, consts]
                    tsl = slice(tt * TT, (tt + 1) * TT)
                    if not c.split:
                        stat_g = gwt[:, k * 8:(k + 1) * 8]
                        mov_g = xhat[:, k, tsl]
                    else:
                        stat_g = (gwt_h[:, k * 8:(k + 1) * 8],
                                  gwt_l[:, k * 8:(k + 1) * 8])
                        mov_g = (xh_h[:, k, tsl], xh_l[:, k, tsl])
                    mm3(psum[0:8, tt, :TT], stat_g, mov_g,
                        start=(k == 0), stop=(k == KC - 1), w=wts,
                        i=[(sPE, 1)] if k == KC - 1 else ())
                p.on("vector", "tensor_scalar_add",
                     lgp[0:8, tt * TT:(tt + 1) * TT], psum[0:8, tt, :TT],
                     gbt[0:8, 0:1], w=[(sPE, sPE.n), ffr], i=[(sDV, 1)])
                p.bank_done(tt, sDV)
            lg_done = (sDV, sDV.n)
            for j in range(CH):
                bk = 6 + (j % 2)
                p.on("tensor", "matmul", psum[0:128, bk, 0:8],
                     lgp[0:8, j * 128:(j + 1) * 128], ident[0:8, 0:8],
                     is_transpose=True, start=True, stop=True,
                     w=p.bank_w(bk) + [lg_done], i=[(sPE, 1)])
                p.on("vector", "tensor_copy", gtmp[:, j, :],
                     psum[0:128, bk, 0:8], w=[(sPE, sPE.n)], i=[(sDV, 1)])
                p.bank_done(bk, sDV)
            p.on("vector", "memset", cwfm[0:9, :], 1.0, i=[(sDV, 1)])
            cw_done_c = [None]
            cwfm_done = [None]

            sm_q = list(range(CH))

            def emit_softmax_j(j):
                g = gtmp[:, j, :]
                kj = cwtok[:, j, :]
                p.on("vector", "tensor_reduce", grow[:, 0:1], g, AXX,
                     OP.max, negate=True, i=[(sDV, 1)])
                p.on("scalar", "activation", g, g, AF.Exp,
                     bias=grow[:, 0:1], w=[(sDV, sDV.n)], i=[(sAC, 1)])
                p.on("vector", "tensor_reduce", grow[:, 0:1], g, AXX,
                     OP.add, w=[(sAC, sAC.n)], i=[(sDV, 1)])
                p.on("vector", "reciprocal", grow[:, 0:1], grow[:, 0:1],
                     i=[(sDV, 1)])
                p.on("vector", "tensor_scalar_mul", g, g, grow[:, 0:1],
                     i=[(sDV, 1)])
                p.on("vector", "tensor_reduce", grow[:, 0:1], g, AXX,
                     OP.max, i=[(sDV, 1)])
                p.on("vector", "tensor_single_scalar", kj, g, grow[:, 0:1],
                     OP.is_ge, i=[(sDV, 1)])
                p.on("vector", "scalar_tensor_tensor", kj, kj, 1e30, g,
                     OP.mult, OP.subtract, i=[(sDV, 1)])
                p.on("vector", "tensor_reduce", grow[:, 1:2], kj, AXX,
                     OP.min, i=[(sDV, 1)])
                p.on("vector", "tensor_scalar", kj, g, grow[:, 1:2], 0.0,
                     OP.add, OP.is_ge, i=[(sDV, 1)])
                p.on("vector", "tensor_tensor", kj, g, kj, OP.mult,
                     i=[(sDV, 1)])

            def emit_softmax():
                while sm_q:
                    emit_softmax_j(sm_q.pop(0))
                cw_done_c[0] = (sDV, sDV.n)

            def emit_gate_back():
                for j in range(CH):
                    bk = 6 + (j % 2)
                    p.on("tensor", "matmul", psum[0:8, bk, 0:128],
                         cwtok[:, j, :], ident[0:128, 0:128],
                         is_transpose=True, start=True, stop=True,
                         w=p.bank_w(bk) + [cw_done_c[0]], i=[(sPE, 1)])
                    p.on("vector", "tensor_copy",
                         cwfm[0:8, j * 128:(j + 1) * 128],
                         psum[0:8, bk, 0:128], w=[(sPE, sPE.n)], i=[(sDV, 1)])
                    p.bank_done(bk, sDV)
                cwfm_done[0] = (sDV, sDV.n)
                p.bar(1)
                if c.debug and res is A:
                    p.dma("gpsimd", io["d_cw"][:, :], cwfm[:, :], dDBG,
                          w=[cwfm_done[0]])
                    p.dbg_guard = (dDBG, dDBG.n)

            silu = sqst
            with contextlib.ExitStack() as estk:
                est = estk.enter_context
                cwbt = est(nc.sbuf_tensor(nm("cwbt"), [128, T], F32))
                ga_n = (HJ2 + 1) // 2
                if not c.split:
                    gbuf = est(nc.sbuf_tensor(nm("gbuf"), [128, HJ2, T], F32))
                    w1sl = est(nc.sbuf_tensor(nm("w1sl"), [128, 3, KC, 128], F32))
                    w2av = U16a[:, :].rearrange("p (j d) -> p j d", d=D)
                    w2bv = U16b[:, :].rearrange("p (j d) -> p j d", d=D)
                    w1st = TileStream(p, "sync",
                                      [w1sl[:, s, :, :] for s in range(3)],
                                      stack, nm("tw1"))
                else:
                    g_h = est(nc.sbuf_tensor(nm("g_h"), [128, HJ2, T], BF16))
                    g_l = est(nc.sbuf_tensor(nm("g_l"), [128, HJ2, T], BF16))
                    w1sl = est(nc.sbuf_tensor(nm("w1sl"),
                                              [128, 2, 2, KC, 128], BF16))
                    w2h = U16ab[:, :].rearrange("p (j d) -> p j d", d=D)
                    w2l = U16bb[:, :].rearrange("p (j d) -> p j d", d=D)
                    w1st = SplitTileStream(p, pump,
                                           [(w1sl[:, s, 0, :, :],
                                             w1sl[:, s, 1, :, :])
                                            for s in range(2)])
                e_order = [8] + list(range(8))
                w1_src = []
                for e_ in e_order:
                    for half_ in range(2):
                        for j_ in range(half_ * HJ2, (half_ + 1) * HJ2):
                            w1_src.append(io["w1s"][e_, j_, :, :, :])
                            w1_src.append(io["w1s"][e_, HJ + j_, :, :, :])
                w1_i = [0]

                def w1_req(n):
                    for _ in range(n):
                        if w1_i[0] < len(w1_src):
                            w1st.request(w1_src[w1_i[0]])
                            w1_i[0] += 1
                w1_req(3 if not c.split else 2)
                gb_n = HJ2 - ga_n
                if not c.split:
                    w2slots = [w2av[:, 0:ga_n, :]] + \
                        ([w2bv[:, 0:gb_n, :]] if gb_n else [])
                    w2st = TileStream(p, "sync", w2slots, stack, nm("tw2"))
                    for si in range(len(w2slots)):
                        w2st.guard[si] = xhat_done
                else:
                    w2st = SplitTileStream(
                        p, pump, [(w2h[:, j_, :], w2l[:, j_, :])
                                  for j_ in range(HJ2)])
                    for si in range(HJ2):
                        w2st.guard[si] = xhat_done
                cw_guard = [None]

                def emit_cwbt(e_):
                    for tt in range(NT):
                        bk = 4 + tt
                        p.on("tensor", "matmul", psum[:, bk, :TT],
                             sel9t[0:9, e_ * 128:(e_ + 1) * 128],
                             cwfm[0:9, tt * TT:(tt + 1) * TT],
                             start=True, stop=True,
                             w=p.bank_w(bk) + [cwfm_done[0], sel_r,
                                               cw_guard[0]],
                             i=[(sPE, 1)])
                    mmn = (sPE, sPE.n)
                    for tt in range(NT):
                        bk = 4 + tt
                        p.on("vector", "tensor_copy",
                             cwbt[:, tt * TT:(tt + 1) * TT],
                             psum[:, bk, :TT], w=[mmn], i=[(sDV, 1)])
                        p.bank_done(bk, sDV)
                    return (sDV, sDV.n)
                for e in e_order:
                    cw_rdy = emit_cwbt(e) if e < 8 else None
                    g_done_tt = [None] * NT
                    p.bar(2)
                    for half in range(2):
                        jlist = list(range(half * HJ2, (half + 1) * HJ2))
                        j0 = half * HJ2
                        if not c.split:
                            w2st.request(io["w2s"][e, j0:j0 + ga_n, :, :]
                                         .rearrange("j p d -> p j d"))
                            _sa, w2ra = w2st.take()
                            if gb_n:
                                w2st.request(
                                    io["w2s"][e, j0 + ga_n:j0 + HJ2, :, :]
                                    .rearrange("j p d -> p j d"))
                                _sb, w2rb = w2st.take()
                            else:
                                w2rb = w2ra
                        else:
                            w2r_l = []
                            for j_ in range(HJ2):
                                w2st.request(io["w2s"][e, j0 + j_, :, :]
                                             .rearrange("p d -> p d"))
                                w2r_l.append(w2st.take()[1])
                            w2ra = w2r_l[0]
                            w2rb = w2r_l[-1]
                        for ji, j in enumerate(jlist):
                            sa, ra = w1st.take()
                            sb_, rb = w1st.take()
                            for tt in range(NT):
                                tsl = slice(tt * TT, (tt + 1) * TT)

                                def _stat(sl_i, k):
                                    if not c.split:
                                        return w1sl[:, sl_i, k, :]
                                    return (w1sl[:, sl_i, 0, k, :],
                                            w1sl[:, sl_i, 1, k, :])

                                def _mov(k):
                                    if not c.split:
                                        return xhat[:, k, tsl]
                                    return (xh_h[:, k, tsl], xh_l[:, k, tsl])
                                fg = e == e_order[0] and half == 0 and ji == 0
                                wts = p.bank_w(tt) + ([ra] if tt == 0 else [])
                                if tt == 0 and not fg:
                                    wts = wts + [xhat_done]
                                for k in range(KC):
                                    kw = wts if k == 0 else []
                                    if fg:
                                        kw = kw + [xhat_f[k]]
                                    mm3(psum[:, tt, :TT], _stat(sa, k), _mov(k),
                                        start=(k == 0), stop=(k == KC - 1),
                                        w=kw)
                                wts = p.bank_w(NT + tt) + ([rb] if tt == 0 else [])
                                for k in range(KC):
                                    kw = wts if k == 0 else []
                                    if fg:
                                        kw = kw + [xhat_f[k]]
                                    mm3(psum[:, NT + tt, :TT], _stat(sb_, k),
                                        _mov(k),
                                        start=(k == 0), stop=(k == KC - 1),
                                        w=kw,
                                        i=[(sPE, 1)] if k == KC - 1 else ())
                                mmv = (sPE, sPE.n)
                                sl = slice(tt * TT, (tt + 1) * TT)
                                ca = e * 2 * HJ + j
                                cb = e * 2 * HJ + HJ + j
                                p.on("scalar", "activation", silu[:, sl],
                                     psum[:, tt, :TT], AF.Silu,
                                     bias=b1s[:, ca:ca + 1],
                                     w=[mmv, ffr, (sDV, sDV.n)],
                                     i=[(sAC, 1)])
                                if e < 8:
                                    p.on("vector", "tensor_tensor", silu[:, sl],
                                         silu[:, sl], cwbt[:, sl], OP.mult,
                                         w=[(sAC, sAC.n), cw_rdy], i=[(sDV, 1)])
                                    wg = [(sDV, sDV.n)]
                                else:
                                    wg = [(sAC, sAC.n)]
                                if not c.split:
                                    p.on("vector", "scalar_tensor_tensor",
                                         gbuf[:, ji, sl], psum[:, NT + tt, :TT],
                                         b1s[:, cb:cb + 1], silu[:, sl],
                                         OP.add, OP.mult, w=wg, i=[(sDV, 1)])
                                else:
                                    p.on("vector", "scalar_tensor_tensor",
                                         silu[:, sl], psum[:, NT + tt, :TT],
                                         b1s[:, cb:cb + 1], silu[:, sl],
                                         OP.add, OP.mult, w=wg, i=[(sDV, 1)])
                                    split2(g_h[:, ji, sl], g_l[:, ji, sl],
                                           silu[:, sl])
                                p.bank_done(tt, sDV)
                                p.bank_done(NT + tt, sDV)
                                g_done_tt[tt] = (sDV, sDV.n)
                                if e == 8 and half == 0 and sm_q:
                                    emit_softmax_j(sm_q.pop(0))
                            w1st.done(sa, sPE)
                            w1st.done(sb_, sPE)
                            w1_req(2)
                        if e == 8 and half == 0:
                            emit_softmax()
                        for tt in range(NT):
                            for oc in range(c.OCF):
                                bk = (oc % 2) * NT + tt
                                wts = p.bank_w(bk) + [w2ra, w2rb, g_done_tt[tt]]
                                osl = slice(oc * 128, (oc + 1) * 128)
                                tsl = slice(tt * TT, (tt + 1) * TT)
                                for ji in range(HJ2):
                                    if not c.split:
                                        w2t = (w2av if ji < ga_n else w2bv)
                                        jj = ji if ji < ga_n else ji - ga_n
                                        stat_o = w2t[:, jj, osl]
                                        mov_o = gbuf[:, ji, tsl]
                                    else:
                                        stat_o = (w2h[:, ji, osl],
                                                  w2l[:, ji, osl])
                                        mov_o = (g_h[:, ji, tsl],
                                                 g_l[:, ji, tsl])
                                    mm3(psum[:, bk, :TT], stat_o, mov_o,
                                        start=(ji == 0), stop=(ji == HJ2 - 1),
                                        w=wts if ji == 0 else [],
                                        i=[(sPE, 1)] if ji == HJ2 - 1 else ())
                                p.on("vector", "scalar_tensor_tensor",
                                     res[:, oc, tt * TT:(tt + 1) * TT],
                                     psum[:, bk, :TT], 0.5,
                                     res[:, oc, tt * TT:(tt + 1) * TT],
                                     OP.mult, OP.add, w=[(sPE, sPE.n)],
                                     i=[(sDV, 1)])
                                p.bank_done(bk, sDV)
                        if not c.split:
                            w2st.done(0, sPE)
                            if gb_n:
                                w2st.done(1, sPE)
                        else:
                            for j_ in range(HJ2):
                                w2st.done(j_, sPE)
                    if e == 8:
                        emit_gate_back()
                    cw_guard[0] = (sDV, sDV.n)
                b2v = sel9t
                dB2 = p.sem(stack, nm("db2"))
                p.dma("sync", b2v[0:9, 0:D], io["b2stack"][:, :], dB2,
                      w=[(sPE, sPE.n)])
                b2r = (dB2, 16)
                for oc in range(c.OCF):
                    for tt in range(NT):
                        bk = (oc % 2) * NT + tt
                        p.on("tensor", "matmul", psum[:, bk, :TT],
                             b2v[0:9, oc * 128:(oc + 1) * 128],
                             cwfm[0:9, tt * TT:(tt + 1) * TT],
                             start=True, stop=True,
                             w=p.bank_w(bk) + [b2r, cwfm_done[0]],
                             i=[(sPE, 1)])
                        p.on("vector", "scalar_tensor_tensor",
                             res[:, oc, tt * TT:(tt + 1) * TT],
                             psum[:, bk, :TT], 0.5,
                             res[:, oc, tt * TT:(tt + 1) * TT],
                             OP.mult, OP.add, w=[(sPE, sPE.n)], i=[(sDV, 1)])
                        p.bank_done(bk, sDV)
                    if out_cb is not None:
                        out_cb(oc, (sDV, sDV.n))

    emit_ffn(A, Bt, x_ready, ready_f=x_chunk)
    p.barrier()
    dump("d_x1", A[:, :, :])

    ln_stats(lambda f: A[:, f, :], (sDV, sDV.n))
    with nc.sbuf_tensor("ainst", [128, 2, T], F32) as ainst:
        la = SpillLane(p, "scalar", stack, "la", 2)
        for f in range(KC):
            s = f % 2
            wts = [la.write_wait(s)]
            p.on("vector", "tensor_tensor", ainst[:, s, :], A[:, f, :],
                 U16a[:, :], OP.mult, w=wts, i=[(sDV, 1)])
            p.on("vector", "tensor_tensor", ainst[:, s, :],
                 ainst[:, s, :], U16b[:, :], OP.add, i=[(sDV, 1)])
            av = (sDV, sDV.n)
            la.spill(s, io["scr_ain"][:, f, :], ainst[:, s, :], w=[av])
        p.on("vector", "tensor_tensor", sqst[:, :], U16a[:, :],
             U16a[:, :], OP.mult, w=[(sPE, sPE.n)], i=[(sDV, 1)])
        p.on("vector", "tensor_scalar", sqst[:, :], sqst[:, :],
             -c.EPS, 1.0, OP.mult, OP.add, i=[(sDV, 1)])
        p.on("scalar", "activation", sqst[:, :], sqst[:, :],
             AF.Sqrt, bias=epst[:, 0:1], w=[(sDV, sDV.n)], i=[(sAC, 1)])
        p.on("vector", "reciprocal", sqst[:, :], sqst[:, :],
             w=[(sAC, sAC.n)], i=[(sDV, 1)])
        p.on("vector", "tensor_tensor", U16b[:, :], U16b[:, :],
             sqst[:, :], OP.mult, i=[(sDV, 1)])
        p.on("vector", "tensor_tensor", U16a[:, :], U16a[:, :],
             sqst[:, :], OP.mult, i=[(sDV, 1)])
    p.barrier()
    if c.debug:
        p.dma("gpsimd", io["d_ain"][:, :, :], io["scr_ain"][:, :, :], dDBG)
        p.dbg_guard = (dDBG, dDBG.n)
    dump("d_u16a", U16a[:, :])
    dump("d_u16b", U16b[:, :])
    h1st = (sDV, sDV.n)
    for f in range(KC):
        p.on("vector", "tensor_tensor", sqst[:, :], A[:, f, :],
             U16a[:, :], OP.mult, w=[h1st] if f == 0 else [], i=[(sDV, 1)])
        p.on("vector", "tensor_tensor", sqst[:, :], sqst[:, :],
             U16b[:, :], OP.add, i=[(sDV, 1)])
        if not c.split:
            p.on("vector", "tensor_scalar", Bt[:, f, :], sqst[:, :],
                 lnv[:, 0 * KC + f:0 * KC + f + 1],
                 lnv[:, 1 * KC + f:1 * KC + f + 1], OP.mult, OP.add,
                 i=[(sDV, 1)])
        else:
            p.on("vector", "tensor_scalar", sqst[:, :], sqst[:, :],
                 lnv[:, 0 * KC + f:0 * KC + f + 1],
                 lnv[:, 1 * KC + f:1 * KC + f + 1], OP.mult, OP.add,
                 i=[(sDV, 1)])
            split2(Bh[:, f, :], Bl[:, f, :], sqst[:, :])
        p.dma("scalar", io["scr_x1"][:, f, :], A[:, f, :], dX1,
              w=[(sDV, sDV.n), (sAC, sAC.n)])
    h1_done = (sDV, sDV.n)
    dump("d_h1", Bt[:, :, :])

    p.barrier()
    HHD, H, CHg = c.HHD, c.H, c.CH
    NG = max(1, HHD // 512)
    GW = HHD // NG
    with contextlib.ExitStack() as astk:
        ast_ = astk.enter_context
        qkst = ast_(nc.sbuf_tensor("qkst", [128, 2, T], F32))
        ktst = ast_(nc.sbuf_tensor("ktst", [128, 2, 128], F32))
        Ssum = ast_(nc.sbuf_tensor("Ssum", [64, HHD], F32))
        if not c.split:
            wvmt = ast_(nc.sbuf_tensor("wvmt", [128, KC, HHD], F32))
            wsl = ast_(nc.sbuf_tensor("wsl", [128, 2, KC, 128], F32))
            p.dma("sync", wvmt[:, :, :],
                  io["wvm"].rearrange("k p m -> p k m"), dVM)
            wvr = (dVM, 16)
        else:
            wvm_h = ast_(nc.sbuf_tensor("wvm_h", [128, KC, HHD], BF16))
            wvm_l = ast_(nc.sbuf_tensor("wvm_l", [128, KC, HHD], BF16))
            wslb = ast_(nc.sbuf_tensor("wslb", [128, 2, 2, KC, 128], BF16))
            for k_ in range(KC):
                wvr = pump.feed(io["wvm"][k_, :, :], wvm_h[:, k_, :],
                                wvm_l[:, k_, :])
        for ch in range(CHg):
            for g in range(NG):
                bk = (ch * NG + g) % 4
                wts = p.bank_w(bk) + ([wvr, h1_done] if ch == 0 and g == 0 else [])
                csl = slice(ch * 128, (ch + 1) * 128)
                gsl = slice(g * GW, (g + 1) * GW)
                for k in range(KC):
                    if not c.split:
                        st_v = Bt[:, k, csl]
                        mv_v = wvmt[:, k, gsl]
                    else:
                        st_v = (Bh[:, k, csl], Bl[:, k, csl])
                        mv_v = (wvm_h[:, k, gsl], wvm_l[:, k, gsl])
                    mm3(psum[:, bk, :GW], st_v, mv_v,
                        start=(k == 0), stop=(k == KC - 1),
                        w=wts if k == 0 else [],
                        i=[(sPE, 1)] if k == KC - 1 else ())
                p.on("vector", "tensor_copy", Av[:, ch, g * GW:(g + 1) * GW],
                     psum[:, bk, :GW], w=[(sPE, sPE.n)], i=[(sDV, 1)])
                p.bank_done(bk, sDV)
        p.bar(1)
        lq = SpillLane(p, "scalar", stack, "lq", 2)
        if not c.split:
            wst = TileStream(p, "sync", [wsl[:, s, :, :] for s in range(2)],
                             stack, "twqk")
        else:
            wst = SplitTileStream(p, pump,
                                  [(wslb[:, s, 0, :, :], wslb[:, s, 1, :, :])
                                   for s in range(2)])
        wst.request(io["wks"][0, :, :, :])
        for m in range(KC):
            if m + 1 < KC:
                wst.request(io["wks"][m + 1, :, :, :])
            else:
                wst.request(io["wqs"][0, :, :, :])
            s_, rdy = wst.take()
            ln_ = (KC + m) % 2
            for tt in range(NT):
                bk = tt % 4
                wts = p.bank_w(bk) + ([rdy] if tt == 0 else [])
                tsl = slice(tt * TT, (tt + 1) * TT)
                for k in range(KC):
                    if not c.split:
                        st_q = wsl[:, s_, k, :]
                        mv_q = Bt[:, k, tsl]
                    else:
                        st_q = (wslb[:, s_, 0, k, :], wslb[:, s_, 1, k, :])
                        mv_q = (Bh[:, k, tsl], Bl[:, k, tsl])
                    mm3(psum[:, bk, :TT], st_q, mv_q,
                        start=(k == 0), stop=(k == KC - 1),
                        w=wts if k == 0 else [],
                        i=[(sPE, 1)] if k == KC - 1 else ())
                p.on("vector", "tensor_copy",
                     qkst[:, ln_, tt * TT:(tt + 1) * TT],
                     psum[:, bk, :TT],
                     w=[(sPE, sPE.n)] + ([lq.write_wait(ln_)] if tt == 0 else []),
                     i=[(sDV, 1)])
                p.bank_done(bk, sDV)
            wst.done(s_, sPE)
            kfm_done = (sDV, sDV.n)
            lq.spill(ln_, io["scr_kfm"][m, :, :], qkst[:, ln_, :],
                     w=[kfm_done])
            for ch in range(CHg):
                ks = ch % 2
                bk = 6 + ks
                p.on("tensor", "matmul", psum[:, bk, 0:128],
                     qkst[:, ln_, ch * 128:(ch + 1) * 128], ident[:, :],
                     is_transpose=True, start=True, stop=True,
                     w=p.bank_w(bk) + [kfm_done], i=[(sPE, 1)])
                p.on("vector", "tensor_copy", ktst[:, ks, :],
                     psum[:, bk, 0:128], w=[(sPE, sPE.n)], i=[(sDV, 1)])
                p.bank_done(bk, sDV)
                ktv = (sDV, sDV.n)
                for h2 in range(2):
                    h = 2 * m + h2
                    sbk = 4 + h // 8
                    start = (h % 8) == 0 and ch == 0
                    stop = ((h % 8) == 7 or h == H - 1) and ch == CHg - 1
                    wts = [ktv]
                    if start:
                        wts += p.bank_w(sbk)
                    p.on("tensor", "matmul",
                         psum[0:64, sbk, (h % 8) * 64:(h % 8) * 64 + 64],
                         ktst[:, ks, h2 * 64:h2 * 64 + 64],
                         Av[:, ch, h * 64:h * 64 + 64],
                         start=start, stop=stop, w=wts,
                         i=[(sPE, 1)])
        p.bar(1)
        sdone = (sPE, sPE.n)
        for sb_i in range(c.NSB):
            n = min(8, H - sb_i * 8) * 64
            p.on("vector", "tensor_copy", Ssum[:, sb_i * 512:sb_i * 512 + n],
                 psum[0:64, 4 + sb_i, 0:n], w=[sdone], i=[(sDV, 1)])
            p.bank_done(4 + sb_i, sDV)
        p.dma("gpsimd", io["cc_in"][:, :], Ssum[:, :], dCC, w=[(sDV, sDV.n)])
        p.wait_only("gpsimd", [(dCC, 16)])
        p.on("gpsimd", "collective_compute", "AllGather", OP.bypass,
             replica_groups=[[0, 1, 2, 3], [4, 5, 6, 7]],
             ins=[io["cc_in"]], outs=[io["cc_out"]], i=[(sCC, 1)])
        for m in range(KC):
            if m + 1 < KC:
                wst.request(io["wqs"][m + 1, :, :, :])
            s_, rdy = wst.take()
            ln_ = m % 2
            for tt in range(NT):
                bk = tt % 4
                wts = p.bank_w(bk) + ([rdy] if tt == 0 else [])
                tsl = slice(tt * TT, (tt + 1) * TT)
                for k in range(KC):
                    if not c.split:
                        st_q = wsl[:, s_, k, :]
                        mv_q = Bt[:, k, tsl]
                    else:
                        st_q = (wslb[:, s_, 0, k, :], wslb[:, s_, 1, k, :])
                        mv_q = (Bh[:, k, tsl], Bl[:, k, tsl])
                    mm3(psum[:, bk, :TT], st_q, mv_q,
                        start=(k == 0), stop=(k == KC - 1),
                        w=wts if k == 0 else [],
                        i=[(sPE, 1)] if k == KC - 1 else ())
                p.on("vector", "tensor_copy",
                     qkst[:, ln_, tt * TT:(tt + 1) * TT],
                     psum[:, bk, :TT],
                     w=[(sPE, sPE.n)] + ([lq.write_wait(ln_)] if tt == 0 else []),
                     i=[(sDV, 1)])
                p.bank_done(bk, sDV)
            wst.done(s_, sPE)
            lq.spill(ln_, io["scr_q"][m, :, :], qkst[:, ln_, :],
                     w=[(sDV, sDV.n)])
        if c.debug:
            p.dma("gpsimd", io["d_ssum"][:, :], Ssum[:, :], dDBG,
                  w=[(sDV, sDV.n)])
            p.dbg_guard = (dDBG, dDBG.n)

    p.barrier(exclude=(sCC,))
    with nc.sbuf_tensor("Mstart", [64, HHD], F32) as Mstart, \
            nc.sbuf_tensor("s_tril", [128, 128], F32) as tril, \
            nc.sbuf_tensor("s_pmask", [64, 4], F32) as pmask, \
            nc.sbuf_tensor("m2r", [64, 64], F32) as m2r:
        dTR = p.sem(stack, "dtr")
        p.dma("sync", tril[:, :], io["tril"][:, :], dTR,
              w=[(sPE, sPE.n), (sDV, sDV.n), (sAC, sAC.n)])
        p.dma("sync", pmask[:, :], io["pmask"][:, :], dTR)
        tri_r = (dTR, 32)
        p.on("vector", "memset", Mstart[:, :], 0.0, i=[(sDV, 1)])
        p.wait_only("gpsimd", [(sCC, sCC.n)])
        with nc.sbuf_tensor("Sj", [64, 2, HHD], F32) as Sj:
            lj = SpillLane(p, "gpsimd", stack, "lj", 2)
            for j in range(4):
                lj.spill(j % 2, Sj[:, j % 2, :], io["cc_out"][j, :, :],
                         w=[(sDV, sDV.n) if j >= 2 else None])
                p.on("vector", "scalar_tensor_tensor", Mstart[:, :],
                     Sj[:, j % 2, :], pmask[:, j:j + 1], Mstart[:, :],
                     OP.mult, OP.add, w=[lj.total(j % 2), tri_r],
                     i=[(sDV, 1)])
        ms_done = (sDV, sDV.n)
        if c.debug:
            p.dma("gpsimd", io["d_mstart"][:, :], Mstart[:, :], dDBG,
                  w=[ms_done])
            p.dbg_guard = (dDBG, dDBG.n)
        stack2 = stack.enter_context(contextlib.ExitStack())
        rlq = stack2.enter_context(nc.sbuf_tensor("rlq", [64, 4, T], F32))
        ktal = stack2.enter_context(nc.sbuf_tensor("ktal", [128, 2, 64], F32))
        yst = stack2.enter_context(nc.sbuf_tensor("yst", [64, 1, T], F32))
        msk = stack2.enter_context(nc.sbuf_tensor("msk", [128, CH, 128], F32))
        rst = TileStream(p, "scalar", [rlq[:, s, :] for s in range(4)],
                         stack, "trl")
        spq = [lq.total(0), lq.total(1)]
        ly = SpillLane(p, "sync", stack, "ly", 1)
        sj_free = [(sDV, sDV.n)]
        rst.request(io["scr_q"][0, 0:64, :], extra_w=spq + sj_free)
        rst.request(io["scr_kfm"][0, 0:64, :], extra_w=spq + sj_free)
        if H > 1:
            rst.request(io["scr_q"][0, 64:128, :], extra_w=spq + sj_free)
            rst.request(io["scr_kfm"][0, 64:128, :], extra_w=spq + sj_free)
        kt_guard = [None, None]
        for h in range(H):
            p.bar(2)
            qs, qr = rst.take()
            ks_, kr = rst.take()
            qt = rlq[:, qs, :]
            kt = rlq[:, ks_, :]
            for ch in range(CH):
                sbk = (ch % 2)
                p.on("tensor", "matmul", psum[0:128, sbk, 0:128],
                     kt[0:64, ch * 128:(ch + 1) * 128],
                     qt[0:64, ch * 128:(ch + 1) * 128],
                     start=True, stop=True,
                     w=p.bank_w(sbk) + ([qr, kr] if ch == 0 else []),
                     i=[(sPE, 1)])
                p.on("vector", "tensor_tensor", msk[:, ch, :],
                     psum[0:128, sbk, 0:128], tril[:, :], OP.mult,
                     w=[(sPE, sPE.n), tri_r], i=[(sDV, 1)])
                p.bank_done(sbk, sDV)
            sc_done = (sDV, sDV.n)
            p.on("vector", "tensor_copy", m2r[:, :],
                 Mstart[0:64, h * 64:h * 64 + 64],
                 w=[ms_done, (sPE, sPE.n)], i=[(sDV, 1)])
            for ch in range(CH):
                ybk = 2 + (ch % 2)
                m2v = (sDV, sDV.n)
                if ch < CH - 1:
                    ks2 = ch % 2
                    bk = 6 + ks2
                    p.on("tensor", "matmul", psum[0:128, bk, 0:64],
                         kt[0:64, ch * 128:(ch + 1) * 128], ident[0:64, 0:64],
                         is_transpose=True, start=True, stop=True,
                         w=p.bank_w(bk), i=[(sPE, 1)])
                    p.on("vector", "tensor_copy", ktal[:, ks2, :],
                         psum[0:128, bk, 0:64],
                         w=[(sPE, sPE.n), kt_guard[ks2]], i=[(sDV, 1)])
                    p.bank_done(bk, sDV)
                p.on("tensor", "matmul", psum[0:64, ybk, 0:128], m2r[:, :],
                     qt[0:64, ch * 128:(ch + 1) * 128], start=True, stop=False,
                     w=p.bank_w(ybk) + [m2v])
                p.on("tensor", "matmul", psum[0:64, ybk, 0:128],
                     Av[:, ch, h * 64:h * 64 + 64], msk[:, ch, :],
                     start=False, stop=True, w=[sc_done], i=[(sPE, 1)])
                ydst = (Bt[0:64, h // 2, ch * 128:(ch + 1) * 128]
                        if h % 2 == 0 else
                        yst[:, 0, ch * 128:(ch + 1) * 128])
                p.on("scalar", "copy", ydst, psum[0:64, ybk, 0:128],
                     w=[(sPE, sPE.n)]
                     + ([ly.write_wait(0)] if (ch == 0 and h % 2 == 1) else []),
                     i=[(sAC, 1)])
                p.bank_done(ybk, sAC)
                if ch < CH - 1:
                    pbk = 4
                    p.on("tensor", "matmul", psum[0:64, pbk, 0:64],
                         ktal[:, ch % 2, :], Av[:, ch, h * 64:h * 64 + 64],
                         start=True, stop=True,
                         w=p.bank_w(pbk) + [(sDV, sDV.n)], i=[(sPE, 1)])
                    kt_guard[ch % 2] = (sPE, sPE.n)
                    p.on("vector", "tensor_tensor", m2r[:, :], m2r[:, :],
                         psum[0:64, pbk, 0:64], OP.add,
                         w=[(sPE, sPE.n)], i=[(sDV, 1)])
                    p.bank_done(pbk, sDV)
            y_done = (sAC, sAC.n)
            if h % 2 == 1:
                ly.spill(0, Bt[64:128, h // 2, :], yst[:, 0, :], w=[y_done])
            rst.done(qs, sPE)
            rst.done(ks_, sPE)
            nh = h + 2
            if nh < H:
                rst.request(io["scr_q"][nh // 2, (nh % 2) * 64:(nh % 2) * 64 + 64, :],
                            extra_w=spq)
                rst.request(io["scr_kfm"][nh // 2, (nh % 2) * 64:(nh % 2) * 64 + 64, :],
                            extra_w=spq)
        y_all = [(sDV, sDV.n), (sAC, sAC.n), ly.total(0)]
        dump("d_y", Bt[:, :, :], y_all)
        stack2.close()

    p.barrier()
    _raq = ["scalar", "sync", "gpsimd"]
    dRAf = [p.sem(stack, f"dRA{f}") for f in range(KC)]
    for f in range(KC):
        p.dma(_raq[f % 3], A[:, f, :], io["scr_ain"][:, f, :], dRAf[f],
              w=[(sPE, sPE.n), la.total(0), la.total(1)])
    ain_back = None
    ain_back_f = [(dRAf[f], 16) for f in range(KC)]
    with nc.sbuf_tensor("wsl2", [128, 2, KC, 128], F32) as wsl2:
        wst2 = TileStream(p, "sync", [wsl2[:, s, :, :] for s in range(2)],
                          stack, "two")
        wst2.request(io["wos"][0, :, :, :])
        for oc in range(KC):
            if oc + 1 < KC:
                wst2.request(io["wos"][oc + 1, :, :, :])
            s_, rdy = wst2.take()
            for tt in range(NT):
                bk = tt % 4
                wts = p.bank_w(bk) + ([rdy] + y_all if tt == 0 else [])
                for k in range(KC):
                    p.on("tensor", "matmul", psum[:, bk, :TT],
                         wsl2[:, s_, k, :], Bt[:, k, tt * TT:(tt + 1) * TT],
                         start=(k == 0), stop=(k == KC - 1),
                         w=wts if k == 0 else [],
                         i=[(sPE, 1)] if k == KC - 1 else ())
                p.on("vector", "tensor_tensor",
                     A[:, oc, tt * TT:(tt + 1) * TT],
                     A[:, oc, tt * TT:(tt + 1) * TT], psum[:, bk, :TT],
                     OP.add, w=[(sPE, sPE.n), ain_back_f[oc]], i=[(sDV, 1)])
                p.bank_done(bk, sDV)
            wst2.done(s_, sPE)
    x2p_done = (sDV, sDV.n)
    dump("d_x2p", A[:, :, :])

    p.barrier()
    dRXf = [p.sem(stack, f"dRX{f}") for f in range(KC)]
    for f in range(KC):
        p.dma(_raq[f % 3], Bt[:, f, :], io["scr_x1"][:, f, :], dRXf[f],
              w=[(sPE, sPE.n), (sDV, sDV.n), (dX1, dX1.n)])
    for f in range(KC):
        p.on("vector", "tensor_tensor", Bt[:, f, :], Bt[:, f, :],
             A[:, f, :], OP.add,
             w=[(dRXf[f], 16)] + ([x2p_done] if f == 0 else []),
             i=[(sDV, 1)])
    p.dma("scalar", io["scr_acc"][:, :, :], Bt[:, :, :], dACC,
          w=[(sDV, sDV.n)])
    acc_spill = (dACC, 16)
    ln_stats(lambda f: A[:, f, :], x2p_done)
    hst = (sDV, sDV.n)
    for f in range(KC):
        p.on("vector", "tensor_tensor", sqst[:, :], A[:, f, :],
             U16a[:, :], OP.mult,
             w=[hst, acc_spill] if f == 0 else [], i=[(sDV, 1)])
        p.on("vector", "tensor_tensor", sqst[:, :], sqst[:, :],
             U16b[:, :], OP.add, i=[(sDV, 1)])
        if not c.split:
            p.on("vector", "tensor_scalar", Bt[:, f, :], sqst[:, :],
                 lnv[:, 2 * KC + f:2 * KC + f + 1],
                 lnv[:, 3 * KC + f:3 * KC + f + 1], OP.mult, OP.add,
                 i=[(sDV, 1)])
        else:
            p.on("vector", "tensor_scalar", sqst[:, :], sqst[:, :],
                 lnv[:, 2 * KC + f:2 * KC + f + 1],
                 lnv[:, 3 * KC + f:3 * KC + f + 1], OP.mult, OP.add,
                 i=[(sDV, 1)])
            split2(Bh[:, f, :], Bl[:, f, :], sqst[:, :])
    lvl_ready = (sDV, sDV.n)
    h3 = A if c.LEVELS % 2 == 1 else Bt
    res2 = Bt if c.LEVELS % 2 == 1 else A
    dRCf = [p.sem(stack, f"dRC{f}") for f in range(KC)]
    _rcq = ["scalar", "gpsimd"]

    p.barrier()
    CMSHC = c.CMSHC
    with contextlib.ExitStack() as cstk:
        if not c.split:
            hid = cstk.enter_context(
                nc.sbuf_tensor("hid", [128, CMSHC, TT], F32))
            if T >= 2 * D and T >= 2 * KC * 128:
                cwsl = U16b[:, 0:2 * KC * 128].rearrange(
                    "p (s k q) -> p s k q", s=2, q=128)
                cw2l = U16a[:, 0:2 * D].rearrange("p (s d) -> p s d", d=D)
            else:
                cwsl = cstk.enter_context(
                    nc.sbuf_tensor("cwsl", [128, 2, KC, 128], F32))[:, :, :, :]
                cw2l = cstk.enter_context(
                    nc.sbuf_tensor("cw2l", [128, 2, D], F32))[:, :, :]
            c1st = TileStream(p, "sync", [cwsl[:, s, :, :] for s in range(2)],
                              stack, "tc1")
            c2st = TileStream(p, "sync", [cw2l[:, s, :] for s in range(2)],
                              stack, "tc2")
        else:
            hid_h = cstk.enter_context(
                nc.sbuf_tensor("hid_h", [128, CMSHC, TT], BF16))
            hid_l = cstk.enter_context(
                nc.sbuf_tensor("hid_l", [128, CMSHC, TT], BF16))
            if 2 * T >= 2 * 2 * KC * 128 and 2 * T >= 4 * D:
                c1b = U16bb[:, 0:4 * KC * 128].rearrange(
                    "p (s h k q) -> p s h k q", s=2, h=2, q=128)
                c2b = U16ab[:, 0:4 * D].rearrange(
                    "p (s h d) -> p s h d", s=2, d=D)
            else:
                c1b = cstk.enter_context(
                    nc.sbuf_tensor("c1b", [128, 2, 2, KC, 128], BF16))[:, :, :, :, :]
                c2b = cstk.enter_context(
                    nc.sbuf_tensor("c2b", [128, 2, 2, D], BF16))[:, :, :, :]
            c1st = SplitTileStream(p, pump,
                                   [(c1b[:, s, 0, :, :], c1b[:, s, 1, :, :])
                                    for s in range(2)])
            c2st = SplitTileStream(p, pump,
                                   [(c2b[:, s, 0, :], c2b[:, s, 1, :])
                                    for s in range(2)])
        for _st in (c1st, c2st):
            _st.guard[0] = lvl_ready
            _st.guard[1] = lvl_ready
        for lvl in range(c.LEVELS):
            p.bar(2)
            src = Bt if lvl % 2 == 0 else A
            dst = A if lvl % 2 == 0 else Bt
            src_ready = lvl_ready
            for tg in range(NT):
                sl = slice(tg * TT, (tg + 1) * TT)
                if lvl == 0 and tg == 0:
                    c1st.request(io["cw1s"][lvl, 0, :, :, :])
                for hc in range(CMSHC):
                    if hc + 1 < CMSHC:
                        c1st.request(io["cw1s"][lvl, hc + 1, :, :, :])
                    else:
                        c2st.request(io["cw2s"][lvl, 0, :, :])
                        if tg + 1 < NT:
                            c1st.request(io["cw1s"][lvl, 0, :, :, :])
                        elif lvl + 1 < c.LEVELS:
                            c1st.request(io["cw1s"][lvl + 1, 0, :, :, :])
                    s_, rdy = c1st.take()
                    bk = hc % 2
                    wts = p.bank_w(bk) + [rdy] + \
                        ([src_ready] if hc == 0 else [])
                    for k in range(KC):
                        if not c.split:
                            st_c = cwsl[:, s_, k, :]
                            mv_c = src[:, k, sl]
                        else:
                            st_c = (c1b[:, s_, 0, k, :], c1b[:, s_, 1, k, :])
                            sh_, sl2_ = hilo[id(src)]
                            mv_c = (sh_[:, k, sl], sl2_[:, k, sl])
                        mm3(psum[:, bk, :TT], st_c, mv_c,
                            start=(k == 0), stop=(k == KC - 1),
                            w=wts if k == 0 else [],
                            i=[(sPE, 1)] if k == KC - 1 else ())
                    cbi = lvl * CMSHC + hc
                    if not c.split:
                        p.on("scalar", "activation", hid[:, hc, :TT],
                             psum[:, bk, :TT], AF.Gelu_apprx_tanh,
                             bias=cb1s[:, cbi:cbi + 1], w=[(sPE, sPE.n)],
                             i=[(sAC, 1)])
                        p.bank_done(bk, sAC)
                    else:
                        p.on("scalar", "activation", sqst[:, 0:TT],
                             psum[:, bk, :TT], AF.Gelu_apprx_tanh,
                             bias=cb1s[:, cbi:cbi + 1],
                             w=[(sPE, sPE.n), (sDV, sDV.n)], i=[(sAC, 1)])
                        split2(hid_h[:, hc, :TT], hid_l[:, hc, :TT],
                               sqst[:, 0:TT], w=[(sAC, sAC.n)])
                        p.bank_done(bk, sDV)
                    c1st.done(s_, sPE)
                hid_done = (sDV, sDV.n) if c.split else (sAC, sAC.n)
                if lvl == c.LEVELS - 1 and tg == NT - 1:
                    for f in range(KC):
                        p.dma(_rcq[f % 2], res2[:, f, :],
                              io["scr_acc"][:, f, :], dRCf[f],
                              w=[(sPE, sPE.n), (sDV, sDV.n), acc_spill])
                for hc in range(CMSHC):
                    if hc + 1 < CMSHC:
                        c2st.request(io["cw2s"][lvl, hc + 1, :, :])
                    s_, rdy = c2st.take()
                    for oc in range(KC):
                        wts = [rdy] if oc == 0 else []
                        if hc == 0:
                            wts = wts + p.bank_w(oc) + \
                                ([hid_done] if oc == 0 else [])
                        osl = slice(oc * 128, (oc + 1) * 128)
                        if not c.split:
                            st_c = cw2l[:, s_, osl]
                            mv_c = hid[:, hc, :TT]
                        else:
                            st_c = (c2b[:, s_, 0, osl], c2b[:, s_, 1, osl])
                            mv_c = (hid_h[:, hc, :TT], hid_l[:, hc, :TT])
                        mm3(psum[:, oc, :TT], st_c, mv_c, start=(hc == 0),
                            stop=(hc == CMSHC - 1), w=wts,
                            i=[(sPE, 1)] if oc == KC - 1 else ())
                    c2st.done(s_, sPE)
                od = (sPE, sPE.n)
                for oc in range(KC):
                    cbo = lvl * KC + oc
                    if not c.split or lvl == c.LEVELS - 1:
                        p.on("vector", "tensor_scalar_add", dst[:, oc, sl],
                             psum[:, oc, :TT], cb2s[:, cbo:cbo + 1],
                             w=[od], i=[(sDV, 1)])
                    else:
                        p.on("vector", "tensor_scalar_add", sqst[:, 0:TT],
                             psum[:, oc, :TT], cb2s[:, cbo:cbo + 1],
                             w=[od], i=[(sDV, 1)])
                        dh_, dl_ = hilo[id(dst)]
                        split2(dh_[:, oc, sl], dl_[:, oc, sl], sqst[:, 0:TT])
                    p.bank_done(oc, sDV)
            lvl_ready = (sDV, sDV.n)
    for f in range(KC):
        p.on("vector", "tensor_tensor", res2[:, f, :], res2[:, f, :],
             h3[:, f, :], OP.add,
             w=[(dRCf[f], 16)] + ([lvl_ready] if f == 0 else []),
             i=[(sDV, 1)])
    dump("d_x2", res2[:, :, :])

    p.barrier()
    outv = io["out"].rearrange("(k p) t -> p k t", p=128)
    _oq = ["scalar", "sync", "gpsimd"]

    def out_cb(oc, rdy):
        p.dma(_oq[oc % 3], outv[:, oc, :], res2[:, oc, :], dOUT, w=[rdy])
    emit_ffn(res2, h3, (sDV, sDV.n), out_cb=out_cb)
    p.wait_only("scalar", [(dOUT, 16 * KC)])
    if c.debug:
        p.wait_only("gpsimd", [(dDBG, dDBG.n)])



def kernel(**inputs):
    cfg = CFG(split=True)
    in_maps = stage_inputs(inputs, cfg)
    nc = build(cfg)
    res = run_bass_kernel_spmd(nc, in_maps, core_ids=list(range(8)))
    out = np.empty((2, 8192, 1024), dtype=np.float32)
    for core in range(8):
        b, seg = core // 4, core % 4
        out[b, seg * cfg.T:(seg + 1) * cfg.T, :] = \
            np.asarray(res.results[core]["out"]).T
    return out


# version 13 (final)
# speedup vs baseline: 1.0211x; 1.0211x over previous
import contextlib
import numpy as np
import concourse.bass as bass
import concourse.mybir as mybir
from concourse.bass_utils import run_bass_kernel_spmd

F32 = mybir.dt.float32
BF16 = mybir.dt.bfloat16
AF = mybir.ActivationFunctionType
OP = mybir.AluOpType
AXX = mybir.AxisListType.X


class CFG:
    def __init__(self, T=2048, D=1024, H=16, F=1024, CMSH=4096, TT=512,
                 split=False, debug=False):
        self.T, self.D, self.H, self.F, self.CMSH, self.TT = T, D, H, F, CMSH, TT
        self.KC = D // 128
        self.HD = 64
        self.CH = T // 128
        self.HJ = F // 128
        self.HJ2 = self.HJ // 2
        self.NE = 9
        self.CMSHC = CMSH // 128
        self.LEVELS = 3
        self.EPS = 1e-5
        self.NT = T // TT
        self.OCF = D // 128
        self.HHD = H * self.HD
        self.NSB = (H + 7) // 8
        self.split = split
        self.debug = debug
        self.barlvl = 0


class Sem:
    __slots__ = ("h", "n", "name")

    def __init__(self, h, name):
        self.h, self.n, self.name = h, 0, name


class Prog:
    ENGS = ("tensor", "vector", "scalar", "sync", "gpsimd")

    def __init__(self, nc):
        self.nc = nc
        self.q = {e: [] for e in self.ENGS}
        self.bank_guard = [None] * 8
        self.self_sem = {}
        self.dbg_guard = None
        self.all_sems = []

    def sem(self, stack, name):
        s = Sem(stack.enter_context(self.nc.semaphore(name)), name)
        self.all_sems.append(s)
        return s

    def barrier(self, exclude=()):
        fence = [(s, s.n) for s in self.all_sems if s.n > 0 and s not in exclude]
        for e in self.ENGS:
            self.on(e, None, w=fence)

    def bar(self, lvl):
        if getattr(self, "barlvl", 0) >= lvl:
            self.barrier()

    def on(self, eng, fn, *args, w=(), i=(), **kw):
        i = list(i)
        w = list(w)
        if self.dbg_guard is not None:
            w.append(self.dbg_guard)
        waits = []
        ss = self.self_sem.get(eng)
        if ss is not None and fn is not None and fn != "dma_start":
            if ss.n > 0:
                waits.append((ss.h, ss.n))
            if not any(s is ss for s, _ in i):
                i.append((ss, 1))
        for x in w:
            if x is None:
                continue
            s, v = x
            assert v <= s.n, f"wait on future {s.name}: {v} > {s.n}"
            if v > 0 and (s is not ss or fn is None or fn == "dma_start"):
                waits.append((s.h, v))
        incs = []
        for s, amt in i:
            s.n += amt
            incs.append((s.h, amt))
        self.q[eng].append((fn, args, kw, waits, incs))

    def wait_only(self, eng, w):
        self.on(eng, None, w=w)

    def dma(self, queue, out, in_, sem, w=()):
        self.on(queue, "dma_start", w=w, i=[(sem, 16)], out=out, in_=in_)

    def bank_w(self, *banks):
        ws = []
        for b in banks:
            g = self.bank_guard[b]
            if g is not None:
                ws.append(g)
                self.bank_guard[b] = None
        return ws

    def bank_done(self, bank, sem):
        self.bank_guard[bank] = (sem, sem.n)

    def replay(self):
        nc = self.nc
        with nc.Block() as block:
            def make(eng_name):
                items = self.q[eng_name]

                def body(eng):
                    for (fn, args, kw, waits, incs) in items:
                        for h, v in waits:
                            eng.wait_ge(h, v)
                        if fn is None:
                            continue
                        ins = getattr(eng, fn)(*args, **kw)
                        for h, amt in incs:
                            ins.then_inc(h, amt)
                return body
            block.tensor(make("tensor"))
            block.vector(make("vector"))
            block.scalar(make("scalar"))
            block.sync(make("sync"))
            block.gpsimd(make("gpsimd"))


class TileStream:

    def __init__(self, p, queue, slots, stack, name):
        self.p, self.queue, self.slots = p, queue, slots
        self.sems = [p.sem(stack, f"{name}_{i}") for i in range(len(slots))]
        self.uses = [0] * len(slots)
        self.guard = [None] * len(slots)
        self.nxt = 0
        self.pend = []

    def request(self, src_ap, extra_w=()):
        s = self.nxt
        self.nxt = (self.nxt + 1) % len(self.slots)
        w = ([self.guard[s]] if self.guard[s] else []) + list(extra_w)
        self.p.dma(self.queue, self.slots[s], src_ap, self.sems[s], w=w)
        self.uses[s] += 1
        self.guard[s] = None
        self.pend.append((s, (self.sems[s], 16 * self.uses[s])))

    def take(self):
        return self.pend.pop(0)

    def done(self, s, sem):
        self.guard[s] = (sem, sem.n)


class SplitPump:

    def __init__(self, p, stack, stage_ap, sDV, sAC, queue="sync"):
        self.p, self.sDV, self.queue = p, sDV, queue
        self.stages = [stage_ap[:, 0, :], stage_ap[:, 1, :]]
        self.sAC = sAC
        self.sems = [p.sem(stack, "dspl0"), p.sem(stack, "dspl1")]
        self.uses = [0, 0]
        self.guards = [None, None]
        self.total = 0

    @staticmethod
    def _halve(ap):
        c = ap.ap[1][1]
        assert c % 2 == 0
        return ap[:, 0:c // 2], ap[:, c // 2:c]

    def feed(self, src_f32, hi_ap, lo_ap, extra_w=()):
        p = self.p
        for src_h, hi_h, lo_h in zip(self._halve(src_f32), self._halve(hi_ap),
                                     self._halve(lo_ap)):
            n = 1
            for st, ct in src_h.ap[1:]:
                n *= ct
            s = self.total % 2
            self.total += 1
            stage = self.stages[s][:, 0:n]
            w = ([self.guards[s]] if self.guards[s] else []) + list(extra_w)
            p.dma(self.queue, stage, src_h, self.sems[s], w=w)
            self.uses[s] += 1
            rdy = (self.sems[s], 16 * self.uses[s])
            p.on("scalar", "copy", hi_h, stage, w=[rdy], i=[(self.sAC, 1)])
            p.on("vector", "tensor_tensor", lo_h, stage, hi_h, OP.subtract,
                 w=[(self.sAC, self.sAC.n)], i=[(self.sDV, 1)])
            self.guards[s] = (self.sDV, self.sDV.n)
        return self.guards[s]


class SplitTileStream:

    def __init__(self, p, pump, slots):
        self.p, self.pump, self.slots = p, pump, slots
        self.guard = [None] * len(slots)
        self.nxt = 0
        self.pend = []

    def request(self, src_ap, extra_w=()):
        s = self.nxt
        self.nxt = (self.nxt + 1) % len(self.slots)
        g = ([self.guard[s]] if self.guard[s] else []) + list(extra_w)
        rdy = self.pump.feed(src_ap, self.slots[s][0], self.slots[s][1],
                             extra_w=g)
        self.guard[s] = None
        self.pend.append((s, rdy))

    def take(self):
        return self.pend.pop(0)

    def done(self, s, sem):
        self.guard[s] = (sem, sem.n)


class SpillLane:

    def __init__(self, p, queue, stack, name, n):
        self.p, self.queue = p, queue
        self.sems = [p.sem(stack, f"{name}_{i}") for i in range(n)]
        self.uses = [0] * n

    def write_wait(self, s):
        return (self.sems[s], 16 * self.uses[s]) if self.uses[s] else None

    def spill(self, s, dst, src, w=()):
        self.p.dma(self.queue, dst, src, self.sems[s], w=w)
        self.uses[s] += 1

    def total(self, s):
        return (self.sems[s], 16 * self.uses[s])



def stage_inputs(inputs, cfg):
    c = cfg
    f32 = np.float32
    cc = lambda a: np.ascontiguousarray(a, dtype=f32)
    x = np.asarray(inputs["L0_attn_x"])
    KC, HJ, D = c.KC, c.HJ, c.D

    w1cat = np.concatenate([np.asarray(inputs["L0_ffn_rw1"]),
                            np.asarray(inputs["L0_ffn_sw1"])], 0)
    w2cat = np.concatenate([np.asarray(inputs["L0_ffn_rw2"]),
                            np.asarray(inputs["L0_ffn_sw2"])], 0)
    b1cat = np.concatenate([np.asarray(inputs["L0_ffn_rb1"]),
                            np.asarray(inputs["L0_ffn_sb1"])], 0)
    b2cat = np.concatenate([np.asarray(inputs["L0_ffn_rb2"]),
                            np.asarray(inputs["L0_ffn_sb2"])], 0)

    w1v = w1cat.reshape(9, KC, 128, 2 * c.F)
    a_p = w1v[:, :, :, :c.F].reshape(9, KC, 128, HJ, 128)
    b_p = w1v[:, :, :, c.F:].reshape(9, KC, 128, HJ, 128)
    w1s = cc(np.concatenate([a_p, b_p], 3).transpose(0, 3, 2, 1, 4))
    w2s = cc(w2cat.reshape(9, HJ, 128, D))
    b1s = cc(b1cat.reshape(9, 2, HJ, 128).transpose(3, 0, 1, 2).reshape(128, 9 * 2 * HJ))
    b2stack = cc(b2cat)

    gw = cc(np.asarray(inputs["L0_ffn_gate_w"]).reshape(KC, 128, 8)
             .transpose(1, 0, 2).reshape(128, KC * 8))
    gb = cc(np.asarray(inputs["L0_ffn_gate_b"]).reshape(8, 1))

    def stat_stage(w):
        M = w.shape[1]
        return cc(w.reshape(KC, 128, M // 128, 128).transpose(2, 1, 0, 3))

    wqs = stat_stage(np.asarray(inputs["L0_attn_wq"]))
    wks = stat_stage(np.asarray(inputs["L0_attn_wk"]))
    wos = stat_stage(np.asarray(inputs["L0_attn_wo"]))
    wvm = cc(np.asarray(inputs["L0_attn_wv"]).reshape(KC, 128, c.HHD))

    lnv = np.stack([np.asarray(inputs["L0_attn_ln1_g"]),
                    np.asarray(inputs["L0_attn_ln1_b"]),
                    np.asarray(inputs["L0_attn_ln2_g"]),
                    np.asarray(inputs["L0_attn_ln2_b"])], 0)
    lnv = cc(lnv.reshape(4, KC, 128).transpose(2, 0, 1).reshape(128, 4 * KC))

    CMSHC = c.CMSHC
    cw1s = cc(np.asarray(inputs["L0_attn_cms_w1"])
              .reshape(3, KC, 128, CMSHC, 128).transpose(0, 3, 2, 1, 4))
    cw2s = cc(np.asarray(inputs["L0_attn_cms_w2"]).reshape(3, CMSHC, 128, D))
    cb1s = cc(np.asarray(inputs["L0_attn_cms_b1"])
              .reshape(3, CMSHC, 128).transpose(2, 0, 1).reshape(128, 3 * CMSHC))
    cb2s = cc(np.asarray(inputs["L0_attn_cms_b2"])
              .reshape(3, KC, 128).transpose(2, 0, 1).reshape(128, 3 * KC))

    ident = np.eye(128, dtype=f32)
    sel9 = np.zeros((9, 8 * 128), dtype=f32)
    for e_ in range(8):
        sel9[e_, e_ * 128:(e_ + 1) * 128] = 1.0
    tril = (np.arange(128)[:, None] <= np.arange(128)[None, :]).astype(f32)

    common = dict(w1s=w1s, w2s=w2s, b1s=b1s, b2stack=b2stack, gw=gw, gb=gb,
                  wqs=wqs, wks=wks, wos=wos, wvm=wvm, lnv=lnv, cw1s=cw1s,
                  cw2s=cw2s, cb1s=cb1s, cb2s=cb2s, ident=ident, tril=tril,
                  sel9=sel9)
    in_maps = []
    for core in range(8):
        b, seg = core // 4, core % 4
        m = dict(common)
        m["x"] = cc(x[b, seg * c.T:(seg + 1) * c.T, :].T)
        pm = np.zeros((64, 4), dtype=f32)
        pm[:, :seg] = 1.0
        m["pmask"] = pm
        in_maps.append(m)
    return in_maps



def build(cfg):
    c = cfg
    nc = bass.Bass("TRN2", target_bir_lowering=False,
                   dynamic_dma_scratch_size=2048)
    T, D, KC, HJ, CMSHC = c.T, c.D, c.KC, c.HJ, c.CMSHC
    dram = lambda n, s, k="ExternalInput": nc.dram_tensor(n, s, F32, kind=k).ap()
    io = {
        "x": dram("x", [D, T]),
        "w1s": dram("w1s", [9, 2 * HJ, 128, KC, 128]),
        "w2s": dram("w2s", [9, HJ, 128, D]),
        "b1s": dram("b1s", [128, 9 * 2 * HJ]),
        "b2stack": dram("b2stack", [9, D]),
        "gw": dram("gw", [128, KC * 8]),
        "gb": dram("gb", [8, 1]),
        "wqs": dram("wqs", [KC, 128, KC, 128]),
        "wks": dram("wks", [KC, 128, KC, 128]),
        "wos": dram("wos", [KC, 128, KC, 128]),
        "wvm": dram("wvm", [KC, 128, c.HHD]),
        "lnv": dram("lnv", [128, 4 * KC]),
        "cw1s": dram("cw1s", [3, CMSHC, 128, KC, 128]),
        "cw2s": dram("cw2s", [3, CMSHC, 128, D]),
        "cb1s": dram("cb1s", [128, 3 * CMSHC]),
        "cb2s": dram("cb2s", [128, 3 * KC]),
        "ident": dram("ident", [128, 128]),
        "sel9": dram("sel9", [9, 8 * 128]),
        "tril": dram("tril", [128, 128]),
        "pmask": dram("pmask", [64, 4]),
        "out": dram("out", [D, T], "ExternalOutput"),
    }
    scr = lambda n, s: nc.dram_tensor(n, s, F32, kind="Internal").ap()
    io["scr_x1"] = scr("scr_x1", [128, KC, T])
    io["scr_ain"] = scr("scr_ain", [128, KC, T])
    io["scr_acc"] = scr("scr_acc", [128, KC, T])
    io["scr_q"] = scr("scr_q", [KC, 128, T])
    io["scr_kfm"] = scr("scr_kfm", [KC, 128, T])
    io["cc_in"] = scr("cc_in", [64, c.HHD])
    io["cc_out"] = scr("cc_out", [4, 64, c.HHD])
    if c.debug:
        for nm in ("d_xhat", "d_x1", "d_h1", "d_x2p", "d_x2"):
            io[nm] = dram(nm, [128, KC, T], "ExternalOutput")
        io["d_cw"] = dram("d_cw", [9, T], "ExternalOutput")
        io["d_ssum"] = dram("d_ssum", [64, c.HHD], "ExternalOutput")
        io["d_mstart"] = dram("d_mstart", [64, c.HHD], "ExternalOutput")
        io["d_y"] = dram("d_y", [128, KC, T], "ExternalOutput")
        io["d_u16a"] = dram("d_u16a", [128, T], "ExternalOutput")
        io["d_u16b"] = dram("d_u16b", [128, T], "ExternalOutput")
        io["d_ain"] = dram("d_ain", [128, KC, T], "ExternalOutput")

    p = Prog(nc)
    p.barlvl = getattr(cfg, "barlvl", 0)
    with contextlib.ExitStack() as stack:
        _body(p, c, io, stack)
        p.replay()
    return nc


def _body(p, c, io, stack):
    nc = p.nc
    T, D, KC, NT, TT, CH, HJ, HJ2 = c.T, c.D, c.KC, c.NT, c.TT, c.CH, c.HJ, c.HJ2
    sb = lambda n, s, dt=F32: stack.enter_context(nc.sbuf_tensor(n, s, dt))

    A = sb("A", [128, KC, T])
    Bt = sb("B", [128, KC, T])
    Av = A[:, :, :].rearrange("p k t -> p (k t)").rearrange(
        "p (ch m) -> p ch m", m=c.HHD)
    psum = stack.enter_context(nc.psum_tensor("ps", [128, 8, 512], F32))
    ones = sb("ones", [128, 128])
    ident = sb("s_ident", [128, 128])
    lnv = sb("s_lnv", [128, 4 * KC])
    cb1s = sb("s_cb1s", [128, 3 * c.CMSHC])
    cb2s = sb("s_cb2s", [128, 3 * KC])
    epst = sb("s_eps", [128, 1])
    U16a = sb("U16a", [128, T])
    U16b = sb("U16b", [128, T])
    sqst = sb("sqst", [128, T])
    if c.split:
        wsplit = sb("wsplit", [128, 2, 512])

        def halves(t):
            fb = t[:, :, :].rearrange("p k t -> p (k t)").bitcast(BF16)
            fb = fb.rearrange("p (s k t) -> p s k t", s=2, t=T)
            return fb[:, 0], fb[:, 1]
        Ah, Al = halves(A)
        Bh, Bl = halves(Bt)
        U16ab = U16a[:, :].bitcast(BF16)
        U16bb = U16b[:, :].bitcast(BF16)
        hilo = {id(A): (Ah, Al), id(Bt): (Bh, Bl)}

    sPE = p.sem(stack, "sPE")
    sDV = p.sem(stack, "sDV")
    sAC = p.sem(stack, "sAC")
    dC = p.sem(stack, "dC")
    dX = p.sem(stack, "dX")
    sCC = p.sem(stack, "sCC")
    dDBG = p.sem(stack, "dDBG")
    dVM = p.sem(stack, "dVM")
    dCC = p.sem(stack, "dCC")
    dX1 = p.sem(stack, "dX1")
    dRA = p.sem(stack, "dRA")
    dRX = p.sem(stack, "dRX")
    dACC = p.sem(stack, "dACC")
    dRC = p.sem(stack, "dRC")
    dOUT = p.sem(stack, "dOUT")
    dY = p.sem(stack, "dY")
    p.self_sem = {"vector": sDV, "scalar": sAC}
    pump = SplitPump(p, stack, wsplit[:, :, :], sDV, sAC) if c.split else None

    def mm3(ps_ap, stat, mov, start, stop, w=(), i=()):
        if not c.split or not isinstance(stat, tuple):
            p.on("tensor", "matmul", ps_ap, stat, mov, start=start, stop=stop,
                 w=w, i=i)
            return
        sh, sl_ = stat
        mh, ml = mov
        p.on("tensor", "matmul", ps_ap, sh, mh, start=start, stop=False, w=w)
        p.on("tensor", "matmul", ps_ap, sh, ml, start=False, stop=False)
        p.on("tensor", "matmul", ps_ap, sl_, mh, start=False, stop=stop, i=i)

    def split2(hi_ap, lo_ap, src_ap, w=()):
        p.on("scalar", "copy", hi_ap, src_ap,
             w=list(w) + [(sDV, sDV.n)], i=[(sAC, 1)])
        p.on("vector", "tensor_tensor", lo_ap, src_ap, hi_ap, OP.subtract,
             w=[(sAC, sAC.n)], i=[(sDV, 1)])

    xv = io["x"].rearrange("(k p) t -> p k t", p=128)
    dXf = [p.sem(stack, f"dXc{f}") for f in range(KC)]
    _xq = ["scalar", "sync", "gpsimd"]
    for f in range(KC):
        p.dma(_xq[f % 3], A[:, f, :], xv[:, f, :], dXf[f])
    x_ready = (dXf[KC - 1], 16)
    x_chunk = [(dXf[f], 16) for f in range(KC)]
    p.dma("sync", lnv[:, :], io["lnv"][:, :], dC)
    p.dma("sync", cb1s[:, :], io["cb1s"][:, :], dC)
    p.dma("sync", cb2s[:, :], io["cb2s"][:, :], dC)
    p.dma("sync", ident[:, :], io["ident"][:, :], dC)
    consts = (dC, dC.n)
    p.on("vector", "memset", ones[:, :], 1.0, i=[(sDV, 1)])
    p.on("vector", "memset", epst[:, :], c.EPS, i=[(sDV, 1)])
    init_done = (sDV, sDV.n)

    def dump(name, ap, extra=()):
        if c.debug:
            p.dbg_guard = None
            p.dma("gpsimd", io[name][:, :] if len(io[name].shape) == 2
                  else io[name][:, :, :], ap, dDBG,
                  w=[(sDV, sDV.n), (sAC, sAC.n), (sPE, sPE.n)] + list(extra))
            p.dbg_guard = (dDBG, dDBG.n)

    def ln_stats(src_f, ready, extra_c1=False, ready_f=None):
        for f in range(KC):
            ap = src_f(f)
            rdy_f = ready_f[f] if ready_f is not None else ready
            wsq = [rdy_f, (sPE, sPE.n)] if ready_f is not None else \
                ([ready, (sPE, sPE.n)] if f == 0 else [(sPE, sPE.n)])
            p.on("scalar", "activation", sqst[:, :], ap, AF.Square,
                 w=wsq, i=[(sAC, 1)])
            for tt in range(NT):
                wts = ([rdy_f, init_done] + p.bank_w(tt)) if f == 0 else \
                    ([rdy_f] if ready_f is not None else [])
                p.on("tensor", "matmul", psum[:, tt, :TT], ones[:, :],
                     ap[:, tt * TT:(tt + 1) * TT], start=(f == 0),
                     stop=(f == KC - 1), w=wts)
            for tt in range(NT):
                wts = [(sAC, sAC.n)]
                if f == 0:
                    wts += p.bank_w(NT + tt)
                p.on("tensor", "matmul", psum[:, NT + tt, :TT], ones[:, :],
                     sqst[:, tt * TT:(tt + 1) * TT], start=(f == 0),
                     stop=(f == KC - 1), w=wts,
                     i=[(sPE, 1)] if tt == NT - 1 else ())
        mm_done = (sPE, sPE.n)
        inv_n = 1.0 / D
        for tt in range(NT):
            sl = slice(tt * TT, (tt + 1) * TT)
            p.on("vector", "tensor_scalar_mul", sqst[:, sl], psum[:, tt, :TT],
                 inv_n, w=[mm_done], i=[(sDV, 1)])
            p.on("vector", "tensor_scalar_mul", U16a[:, sl],
                 psum[:, NT + tt, :TT], inv_n, i=[(sDV, 1)])
            p.on("vector", "scalar_tensor_tensor", U16b[:, sl],
                 sqst[:, sl], -1.0, sqst[:, sl], OP.mult, OP.mult, i=[(sDV, 1)])
            p.on("vector", "tensor_tensor", U16a[:, sl],
                 U16a[:, sl], U16b[:, sl], OP.add, i=[(sDV, 1)])
            p.on("scalar", "activation", U16a[:, sl], U16a[:, sl],
                 AF.Sqrt, bias=epst[:, 0:1], w=[(sDV, sDV.n)], i=[(sAC, 1)])
            p.on("vector", "reciprocal", U16a[:, sl], U16a[:, sl],
                 w=[(sAC, sAC.n)], i=[(sDV, 1)])
            p.on("vector", "scalar_tensor_tensor", U16b[:, sl], sqst[:, sl],
                 -1.0, U16a[:, sl], OP.mult, OP.mult, i=[(sDV, 1)])
            p.bank_done(tt, sDV)
            p.bank_done(NT + tt, sDV)

    def emit_ffn(res, xhat, ready, tag=[0], ready_f=None, out_cb=None):
        tg_ = tag[0]; tag[0] += 1
        nm = lambda s: f"{s}{tg_}"
        ln_stats(lambda f: res[:, f, :], ready, ready_f=ready_f)
        st = (sDV, sDV.n)
        if c.split:
            xh_h, xh_l = hilo[id(xhat)]
        xhat_f = []
        for f in range(KC):
            p.on("vector", "tensor_tensor", sqst[:, :], res[:, f, :],
                 U16a[:, :], OP.mult, w=[st] if f == 0 else [],
                 i=[(sDV, 1)])
            if not c.split:
                p.on("vector", "tensor_tensor", xhat[:, f, :], sqst[:, :],
                     U16b[:, :], OP.add, i=[(sDV, 1)])
            else:
                p.on("vector", "tensor_tensor", sqst[:, :], sqst[:, :],
                     U16b[:, :], OP.add, i=[(sDV, 1)])
                split2(xh_h[:, f, :], xh_l[:, f, :], sqst[:, :])
            xhat_f.append((sDV, sDV.n))
        xhat_done = (sDV, sDV.n)
        p.bar(1)

        lgp = sqst
        ffc = p.sem(stack, nm("dffc"))
        with nc.sbuf_tensor(nm("b1s"), [128, 9 * 2 * HJ], F32) as b1s, \
                nc.sbuf_tensor(nm("gwt"), [128, KC * 8], F32) as gwt, \
                nc.sbuf_tensor(nm("gwt_h"), [128, KC * 8], BF16) as gwt_h, \
                nc.sbuf_tensor(nm("gwt_l"), [128, KC * 8], BF16) as gwt_l, \
                nc.sbuf_tensor(nm("gbt"), [8, 1], F32) as gbt, \
                nc.sbuf_tensor(nm("sel9t"), [9, 8 * 128], F32) as sel9t, \
                nc.sbuf_tensor(nm("cwfm"), [9, T], F32) as cwfm, \
                nc.sbuf_tensor(nm("gtmp"), [128, CH, 8], F32) as gtmp, \
                nc.sbuf_tensor(nm("cwtok"), [128, CH, 8], F32) as cwtok, \
                nc.sbuf_tensor(nm("grow"), [128, 2], F32) as grow:
            dSEL = p.sem(stack, nm("dsel"))
            p.dma("sync", sel9t[:, :], io["sel9"][:, :], dSEL)
            sel_r = (dSEL, 16)
            p.dma("sync", b1s[:, :], io["b1s"][:, :], ffc,
                  w=[(sPE, sPE.n), (sAC, sAC.n)])
            p.dma("sync", gwt[:, :], io["gw"][:, :], ffc)
            p.dma("sync", gbt[:, :], io["gb"][:, :], ffc)
            ffr = (ffc, 48)
            gw_r = ffr
            if c.split:
                split2(gwt_h[:, :], gwt_l[:, :], gwt[:, :], w=[ffr])
                gw_r = (sDV, sDV.n)
            for tt in range(NT):
                for k in range(KC):
                    wts = [xhat_f[k]]
                    if k == 0:
                        wts = p.bank_w(tt) + [xhat_f[0], gw_r, consts]
                    tsl = slice(tt * TT, (tt + 1) * TT)
                    if not c.split:
                        stat_g = gwt[:, k * 8:(k + 1) * 8]
                        mov_g = xhat[:, k, tsl]
                    else:
                        stat_g = (gwt_h[:, k * 8:(k + 1) * 8],
                                  gwt_l[:, k * 8:(k + 1) * 8])
                        mov_g = (xh_h[:, k, tsl], xh_l[:, k, tsl])
                    mm3(psum[0:8, tt, :TT], stat_g, mov_g,
                        start=(k == 0), stop=(k == KC - 1), w=wts,
                        i=[(sPE, 1)] if k == KC - 1 else ())
                p.on("vector", "tensor_scalar_add",
                     lgp[0:8, tt * TT:(tt + 1) * TT], psum[0:8, tt, :TT],
                     gbt[0:8, 0:1], w=[(sPE, sPE.n), ffr], i=[(sDV, 1)])
                p.bank_done(tt, sDV)
            lg_done = (sDV, sDV.n)
            for j in range(CH):
                bk = 6 + (j % 2)
                p.on("tensor", "matmul", psum[0:128, bk, 0:8],
                     lgp[0:8, j * 128:(j + 1) * 128], ident[0:8, 0:8],
                     is_transpose=True, start=True, stop=True,
                     w=p.bank_w(bk) + [lg_done], i=[(sPE, 1)])
                p.on("vector", "tensor_copy", gtmp[:, j, :],
                     psum[0:128, bk, 0:8], w=[(sPE, sPE.n)], i=[(sDV, 1)])
                p.bank_done(bk, sDV)
            p.on("vector", "memset", cwfm[0:9, :], 1.0, i=[(sDV, 1)])
            cw_done_c = [None]
            cwfm_done = [None]

            sm_q = list(range(CH))

            def emit_softmax_j(j):
                g = gtmp[:, j, :]
                kj = cwtok[:, j, :]
                p.on("vector", "tensor_reduce", grow[:, 0:1], g, AXX,
                     OP.max, negate=True, i=[(sDV, 1)])
                p.on("scalar", "activation", g, g, AF.Exp,
                     bias=grow[:, 0:1], w=[(sDV, sDV.n)], i=[(sAC, 1)])
                p.on("vector", "tensor_reduce", grow[:, 0:1], g, AXX,
                     OP.add, w=[(sAC, sAC.n)], i=[(sDV, 1)])
                p.on("vector", "reciprocal", grow[:, 0:1], grow[:, 0:1],
                     i=[(sDV, 1)])
                p.on("vector", "tensor_scalar_mul", g, g, grow[:, 0:1],
                     i=[(sDV, 1)])
                p.on("vector", "tensor_reduce", grow[:, 0:1], g, AXX,
                     OP.max, i=[(sDV, 1)])
                p.on("vector", "tensor_single_scalar", kj, g, grow[:, 0:1],
                     OP.is_ge, i=[(sDV, 1)])
                p.on("vector", "scalar_tensor_tensor", kj, kj, 1e30, g,
                     OP.mult, OP.subtract, i=[(sDV, 1)])
                p.on("vector", "tensor_reduce", grow[:, 1:2], kj, AXX,
                     OP.min, i=[(sDV, 1)])
                p.on("vector", "tensor_scalar", kj, g, grow[:, 1:2], 0.0,
                     OP.add, OP.is_ge, i=[(sDV, 1)])
                p.on("vector", "tensor_tensor", kj, g, kj, OP.mult,
                     i=[(sDV, 1)])

            def emit_softmax():
                while sm_q:
                    emit_softmax_j(sm_q.pop(0))
                cw_done_c[0] = (sDV, sDV.n)

            def emit_gate_back():
                for j in range(CH):
                    bk = 6 + (j % 2)
                    p.on("tensor", "matmul", psum[0:8, bk, 0:128],
                         cwtok[:, j, :], ident[0:128, 0:128],
                         is_transpose=True, start=True, stop=True,
                         w=p.bank_w(bk) + [cw_done_c[0]], i=[(sPE, 1)])
                    p.on("vector", "tensor_copy",
                         cwfm[0:8, j * 128:(j + 1) * 128],
                         psum[0:8, bk, 0:128], w=[(sPE, sPE.n)], i=[(sDV, 1)])
                    p.bank_done(bk, sDV)
                cwfm_done[0] = (sDV, sDV.n)
                p.bar(1)
                if c.debug and res is A:
                    p.dma("gpsimd", io["d_cw"][:, :], cwfm[:, :], dDBG,
                          w=[cwfm_done[0]])
                    p.dbg_guard = (dDBG, dDBG.n)

            silu = sqst
            with contextlib.ExitStack() as estk:
                est = estk.enter_context
                cwbt = est(nc.sbuf_tensor(nm("cwbt"), [128, T], F32))
                ga_n = (HJ2 + 1) // 2
                if not c.split:
                    gbuf = est(nc.sbuf_tensor(nm("gbuf"), [128, HJ2, T], F32))
                    w1sl = est(nc.sbuf_tensor(nm("w1sl"), [128, 3, KC, 128], F32))
                    w2av = U16a[:, :].rearrange("p (j d) -> p j d", d=D)
                    w2bv = U16b[:, :].rearrange("p (j d) -> p j d", d=D)
                    w1st = TileStream(p, "sync",
                                      [w1sl[:, s, :, :] for s in range(3)],
                                      stack, nm("tw1"))
                else:
                    g_h = est(nc.sbuf_tensor(nm("g_h"), [128, HJ2, T], BF16))
                    g_l = est(nc.sbuf_tensor(nm("g_l"), [128, HJ2, T], BF16))
                    w1sl = est(nc.sbuf_tensor(nm("w1sl"),
                                              [128, 2, 2, KC, 128], BF16))
                    w2h = U16ab[:, :].rearrange("p (j d) -> p j d", d=D)
                    w2l = U16bb[:, :].rearrange("p (j d) -> p j d", d=D)
                    w1st = SplitTileStream(p, pump,
                                           [(w1sl[:, s, 0, :, :],
                                             w1sl[:, s, 1, :, :])
                                            for s in range(2)])
                e_order = [8] + list(range(8))
                w1_src = []
                for e_ in e_order:
                    for half_ in range(2):
                        for j_ in range(half_ * HJ2, (half_ + 1) * HJ2):
                            w1_src.append(io["w1s"][e_, j_, :, :, :])
                            w1_src.append(io["w1s"][e_, HJ + j_, :, :, :])
                w1_i = [0]

                def w1_req(n):
                    for _ in range(n):
                        if w1_i[0] < len(w1_src):
                            w1st.request(w1_src[w1_i[0]])
                            w1_i[0] += 1
                w1_req(3 if not c.split else 2)
                gb_n = HJ2 - ga_n
                if not c.split:
                    w2slots = [w2av[:, 0:ga_n, :]] + \
                        ([w2bv[:, 0:gb_n, :]] if gb_n else [])
                    w2st = TileStream(p, "sync", w2slots, stack, nm("tw2"))
                    for si in range(len(w2slots)):
                        w2st.guard[si] = xhat_done
                else:
                    w2st = SplitTileStream(
                        p, pump, [(w2h[:, j_, :], w2l[:, j_, :])
                                  for j_ in range(HJ2)])
                    for si in range(HJ2):
                        w2st.guard[si] = xhat_done
                cw_guard = [None]

                def emit_cwbt(e_):
                    for tt in range(NT):
                        bk = 4 + tt
                        p.on("tensor", "matmul", psum[:, bk, :TT],
                             sel9t[0:9, e_ * 128:(e_ + 1) * 128],
                             cwfm[0:9, tt * TT:(tt + 1) * TT],
                             start=True, stop=True,
                             w=p.bank_w(bk) + [cwfm_done[0], sel_r,
                                               cw_guard[0]],
                             i=[(sPE, 1)])
                    mmn = (sPE, sPE.n)
                    for tt in range(NT):
                        bk = 4 + tt
                        p.on("vector", "tensor_copy",
                             cwbt[:, tt * TT:(tt + 1) * TT],
                             psum[:, bk, :TT], w=[mmn], i=[(sDV, 1)])
                        p.bank_done(bk, sDV)
                    return (sDV, sDV.n)
                for e in e_order:
                    cw_rdy = emit_cwbt(e) if e < 8 else None
                    g_done_tt = [None] * NT
                    p.bar(2)
                    for half in range(2):
                        jlist = list(range(half * HJ2, (half + 1) * HJ2))
                        j0 = half * HJ2
                        if not c.split:
                            w2st.request(io["w2s"][e, j0:j0 + ga_n, :, :]
                                         .rearrange("j p d -> p j d"))
                            _sa, w2ra = w2st.take()
                            if gb_n:
                                w2st.request(
                                    io["w2s"][e, j0 + ga_n:j0 + HJ2, :, :]
                                    .rearrange("j p d -> p j d"))
                                _sb, w2rb = w2st.take()
                            else:
                                w2rb = w2ra
                        else:
                            w2r_l = []
                            for j_ in range(HJ2):
                                w2st.request(io["w2s"][e, j0 + j_, :, :]
                                             .rearrange("p d -> p d"))
                                w2r_l.append(w2st.take()[1])
                            w2ra = w2r_l[0]
                            w2rb = w2r_l[-1]
                        for ji, j in enumerate(jlist):
                            sa, ra = w1st.take()
                            sb_, rb = w1st.take()
                            for tt in range(NT):
                                tsl = slice(tt * TT, (tt + 1) * TT)

                                def _stat(sl_i, k):
                                    if not c.split:
                                        return w1sl[:, sl_i, k, :]
                                    return (w1sl[:, sl_i, 0, k, :],
                                            w1sl[:, sl_i, 1, k, :])

                                def _mov(k):
                                    if not c.split:
                                        return xhat[:, k, tsl]
                                    return (xh_h[:, k, tsl], xh_l[:, k, tsl])
                                fg = e == e_order[0] and half == 0 and ji == 0
                                wts = p.bank_w(tt) + ([ra] if tt == 0 else [])
                                if tt == 0 and not fg:
                                    wts = wts + [xhat_done]
                                for k in range(KC):
                                    kw = wts if k == 0 else []
                                    if fg:
                                        kw = kw + [xhat_f[k]]
                                    mm3(psum[:, tt, :TT], _stat(sa, k), _mov(k),
                                        start=(k == 0), stop=(k == KC - 1),
                                        w=kw)
                                wts = p.bank_w(NT + tt) + ([rb] if tt == 0 else [])
                                for k in range(KC):
                                    kw = wts if k == 0 else []
                                    if fg:
                                        kw = kw + [xhat_f[k]]
                                    mm3(psum[:, NT + tt, :TT], _stat(sb_, k),
                                        _mov(k),
                                        start=(k == 0), stop=(k == KC - 1),
                                        w=kw,
                                        i=[(sPE, 1)] if k == KC - 1 else ())
                                mmv = (sPE, sPE.n)
                                sl = slice(tt * TT, (tt + 1) * TT)
                                ca = e * 2 * HJ + j
                                cb = e * 2 * HJ + HJ + j
                                p.on("scalar", "activation", silu[:, sl],
                                     psum[:, tt, :TT], AF.Silu,
                                     bias=b1s[:, ca:ca + 1],
                                     w=[mmv, ffr, (sDV, sDV.n)],
                                     i=[(sAC, 1)])
                                if e < 8:
                                    p.on("vector", "tensor_tensor", silu[:, sl],
                                         silu[:, sl], cwbt[:, sl], OP.mult,
                                         w=[(sAC, sAC.n), cw_rdy], i=[(sDV, 1)])
                                    wg = [(sDV, sDV.n)]
                                else:
                                    wg = [(sAC, sAC.n)]
                                if not c.split:
                                    p.on("vector", "scalar_tensor_tensor",
                                         gbuf[:, ji, sl], psum[:, NT + tt, :TT],
                                         b1s[:, cb:cb + 1], silu[:, sl],
                                         OP.add, OP.mult, w=wg, i=[(sDV, 1)])
                                else:
                                    p.on("vector", "scalar_tensor_tensor",
                                         silu[:, sl], psum[:, NT + tt, :TT],
                                         b1s[:, cb:cb + 1], silu[:, sl],
                                         OP.add, OP.mult, w=wg, i=[(sDV, 1)])
                                    split2(g_h[:, ji, sl], g_l[:, ji, sl],
                                           silu[:, sl])
                                p.bank_done(tt, sDV)
                                p.bank_done(NT + tt, sDV)
                                g_done_tt[tt] = (sDV, sDV.n)
                                if e == 8 and half == 0 and sm_q:
                                    emit_softmax_j(sm_q.pop(0))
                            w1st.done(sa, sPE)
                            w1st.done(sb_, sPE)
                            w1_req(2)
                        if e == 8 and half == 0:
                            emit_softmax()
                        for tt in range(NT):
                            for oc in range(c.OCF):
                                bk = (oc % 2) * NT + tt
                                wts = p.bank_w(bk) + [w2ra, w2rb, g_done_tt[tt]]
                                osl = slice(oc * 128, (oc + 1) * 128)
                                tsl = slice(tt * TT, (tt + 1) * TT)
                                for ji in range(HJ2):
                                    if not c.split:
                                        w2t = (w2av if ji < ga_n else w2bv)
                                        jj = ji if ji < ga_n else ji - ga_n
                                        stat_o = w2t[:, jj, osl]
                                        mov_o = gbuf[:, ji, tsl]
                                    else:
                                        stat_o = (w2h[:, ji, osl],
                                                  w2l[:, ji, osl])
                                        mov_o = (g_h[:, ji, tsl],
                                                 g_l[:, ji, tsl])
                                    mm3(psum[:, bk, :TT], stat_o, mov_o,
                                        start=(ji == 0), stop=(ji == HJ2 - 1),
                                        w=wts if ji == 0 else [],
                                        i=[(sPE, 1)] if ji == HJ2 - 1 else ())
                                p.on("vector", "scalar_tensor_tensor",
                                     res[:, oc, tt * TT:(tt + 1) * TT],
                                     psum[:, bk, :TT], 0.5,
                                     res[:, oc, tt * TT:(tt + 1) * TT],
                                     OP.mult, OP.add, w=[(sPE, sPE.n)],
                                     i=[(sDV, 1)])
                                p.bank_done(bk, sDV)
                        if not c.split:
                            w2st.done(0, sPE)
                            if gb_n:
                                w2st.done(1, sPE)
                        else:
                            for j_ in range(HJ2):
                                w2st.done(j_, sPE)
                    if e == 8:
                        emit_gate_back()
                    cw_guard[0] = (sDV, sDV.n)
                b2v = sel9t
                dB2 = p.sem(stack, nm("db2"))
                p.dma("sync", b2v[0:9, 0:D], io["b2stack"][:, :], dB2,
                      w=[(sPE, sPE.n)])
                b2r = (dB2, 16)
                for oc in range(c.OCF):
                    for tt in range(NT):
                        bk = (oc % 2) * NT + tt
                        p.on("tensor", "matmul", psum[:, bk, :TT],
                             b2v[0:9, oc * 128:(oc + 1) * 128],
                             cwfm[0:9, tt * TT:(tt + 1) * TT],
                             start=True, stop=True,
                             w=p.bank_w(bk) + [b2r, cwfm_done[0]],
                             i=[(sPE, 1)])
                        p.on("vector", "scalar_tensor_tensor",
                             res[:, oc, tt * TT:(tt + 1) * TT],
                             psum[:, bk, :TT], 0.5,
                             res[:, oc, tt * TT:(tt + 1) * TT],
                             OP.mult, OP.add, w=[(sPE, sPE.n)], i=[(sDV, 1)])
                        p.bank_done(bk, sDV)
                    if out_cb is not None:
                        out_cb(oc, (sDV, sDV.n))

    emit_ffn(A, Bt, x_ready, ready_f=x_chunk)
    p.barrier()
    dump("d_x1", A[:, :, :])

    ln_stats(lambda f: A[:, f, :], (sDV, sDV.n))
    with nc.sbuf_tensor("ainst", [128, 2, T], F32) as ainst:
        la = SpillLane(p, "scalar", stack, "la", 2)
        for f in range(KC):
            s = f % 2
            wts = [la.write_wait(s)]
            p.on("vector", "tensor_tensor", ainst[:, s, :], A[:, f, :],
                 U16a[:, :], OP.mult, w=wts, i=[(sDV, 1)])
            p.on("vector", "tensor_tensor", ainst[:, s, :],
                 ainst[:, s, :], U16b[:, :], OP.add, i=[(sDV, 1)])
            av = (sDV, sDV.n)
            la.spill(s, io["scr_ain"][:, f, :], ainst[:, s, :], w=[av])
        p.on("vector", "tensor_tensor", sqst[:, :], U16a[:, :],
             U16a[:, :], OP.mult, w=[(sPE, sPE.n)], i=[(sDV, 1)])
        p.on("vector", "tensor_scalar", sqst[:, :], sqst[:, :],
             -c.EPS, 1.0, OP.mult, OP.add, i=[(sDV, 1)])
        p.on("scalar", "activation", sqst[:, :], sqst[:, :],
             AF.Sqrt, bias=epst[:, 0:1], w=[(sDV, sDV.n)], i=[(sAC, 1)])
        p.on("vector", "reciprocal", sqst[:, :], sqst[:, :],
             w=[(sAC, sAC.n)], i=[(sDV, 1)])
        p.on("vector", "tensor_tensor", U16b[:, :], U16b[:, :],
             sqst[:, :], OP.mult, i=[(sDV, 1)])
        p.on("vector", "tensor_tensor", U16a[:, :], U16a[:, :],
             sqst[:, :], OP.mult, i=[(sDV, 1)])
    p.barrier()
    if c.debug:
        p.dma("gpsimd", io["d_ain"][:, :, :], io["scr_ain"][:, :, :], dDBG)
        p.dbg_guard = (dDBG, dDBG.n)
    dump("d_u16a", U16a[:, :])
    dump("d_u16b", U16b[:, :])
    h1st = (sDV, sDV.n)
    h1_f = []
    for f in range(KC):
        p.on("vector", "tensor_tensor", sqst[:, :], A[:, f, :],
             U16a[:, :], OP.mult, w=[h1st] if f == 0 else [], i=[(sDV, 1)])
        p.on("vector", "tensor_tensor", sqst[:, :], sqst[:, :],
             U16b[:, :], OP.add, i=[(sDV, 1)])
        if not c.split:
            p.on("vector", "tensor_scalar", Bt[:, f, :], sqst[:, :],
                 lnv[:, 0 * KC + f:0 * KC + f + 1],
                 lnv[:, 1 * KC + f:1 * KC + f + 1], OP.mult, OP.add,
                 i=[(sDV, 1)])
        else:
            p.on("vector", "tensor_scalar", sqst[:, :], sqst[:, :],
                 lnv[:, 0 * KC + f:0 * KC + f + 1],
                 lnv[:, 1 * KC + f:1 * KC + f + 1], OP.mult, OP.add,
                 i=[(sDV, 1)])
            split2(Bh[:, f, :], Bl[:, f, :], sqst[:, :])
        h1_f.append((sDV, sDV.n))
        p.dma("scalar", io["scr_x1"][:, f, :], A[:, f, :], dX1,
              w=[(sDV, sDV.n), (sAC, sAC.n)])
    h1_done = (sDV, sDV.n)
    dump("d_h1", Bt[:, :, :])

    p.barrier()
    HHD, H, CHg = c.HHD, c.H, c.CH
    NG = max(1, HHD // 512)
    GW = HHD // NG
    with contextlib.ExitStack() as astk:
        ast_ = astk.enter_context
        qkst = ast_(nc.sbuf_tensor("qkst", [128, 2, T], F32))
        ktst = ast_(nc.sbuf_tensor("ktst", [128, 2, 128], F32))
        Ssum = ast_(nc.sbuf_tensor("Ssum", [64, HHD], F32))
        if not c.split:
            wvmt = ast_(nc.sbuf_tensor("wvmt", [128, KC, HHD], F32))
            wsl = ast_(nc.sbuf_tensor("wsl", [128, 2, KC, 128], F32))
            p.dma("sync", wvmt[:, :, :],
                  io["wvm"].rearrange("k p m -> p k m"), dVM)
            wvr = (dVM, 16)
        else:
            wvm_h = ast_(nc.sbuf_tensor("wvm_h", [128, KC, HHD], BF16))
            wvm_l = ast_(nc.sbuf_tensor("wvm_l", [128, KC, HHD], BF16))
            wslb = ast_(nc.sbuf_tensor("wslb", [128, 2, 2, KC, 128], BF16))
            for k_ in range(KC):
                wvr = pump.feed(io["wvm"][k_, :, :], wvm_h[:, k_, :],
                                wvm_l[:, k_, :])
        for ch in range(CHg):
            for g in range(NG):
                gi = ch * NG + g
                bk = gi % 4
                wts = p.bank_w(bk) + ([wvr] if gi == 0 else [])
                csl = slice(ch * 128, (ch + 1) * 128)
                gsl = slice(g * GW, (g + 1) * GW)
                for k in range(KC):
                    kw = wts if k == 0 else []
                    if gi < KC:
                        kw = kw + [h1_f[k]]
                    if not c.split:
                        st_v = Bt[:, k, csl]
                        mv_v = wvmt[:, k, gsl]
                    else:
                        st_v = (Bh[:, k, csl], Bl[:, k, csl])
                        mv_v = (wvm_h[:, k, gsl], wvm_l[:, k, gsl])
                    mm3(psum[:, bk, :GW], st_v, mv_v,
                        start=(k == 0), stop=(k == KC - 1),
                        w=kw,
                        i=[(sPE, 1)] if k == KC - 1 else ())
                p.on("vector", "tensor_copy", Av[:, ch, g * GW:(g + 1) * GW],
                     psum[:, bk, :GW], w=[(sPE, sPE.n)], i=[(sDV, 1)])
                p.bank_done(bk, sDV)
        p.bar(1)
        lq = SpillLane(p, "scalar", stack, "lq", 2)
        if not c.split:
            wst = TileStream(p, "sync", [wsl[:, s, :, :] for s in range(2)],
                             stack, "twqk")
        else:
            wst = SplitTileStream(p, pump,
                                  [(wslb[:, s, 0, :, :], wslb[:, s, 1, :, :])
                                   for s in range(2)])
        wst.request(io["wks"][0, :, :, :])
        for m in range(KC):
            if m + 1 < KC:
                wst.request(io["wks"][m + 1, :, :, :])
            else:
                wst.request(io["wqs"][0, :, :, :])
            s_, rdy = wst.take()
            ln_ = (KC + m) % 2
            for tt in range(NT):
                bk = tt % 4
                wts = p.bank_w(bk) + ([rdy] if tt == 0 else [])
                tsl = slice(tt * TT, (tt + 1) * TT)
                for k in range(KC):
                    if not c.split:
                        st_q = wsl[:, s_, k, :]
                        mv_q = Bt[:, k, tsl]
                    else:
                        st_q = (wslb[:, s_, 0, k, :], wslb[:, s_, 1, k, :])
                        mv_q = (Bh[:, k, tsl], Bl[:, k, tsl])
                    mm3(psum[:, bk, :TT], st_q, mv_q,
                        start=(k == 0), stop=(k == KC - 1),
                        w=wts if k == 0 else [],
                        i=[(sPE, 1)] if k == KC - 1 else ())
                p.on("vector", "tensor_copy",
                     qkst[:, ln_, tt * TT:(tt + 1) * TT],
                     psum[:, bk, :TT],
                     w=[(sPE, sPE.n)] + ([lq.write_wait(ln_)] if tt == 0 else []),
                     i=[(sDV, 1)])
                p.bank_done(bk, sDV)
            wst.done(s_, sPE)
            kfm_done = (sDV, sDV.n)
            lq.spill(ln_, io["scr_kfm"][m, :, :], qkst[:, ln_, :],
                     w=[kfm_done])
            for ch in range(CHg):
                ks = ch % 2
                bk = 6 + ks
                p.on("tensor", "matmul", psum[:, bk, 0:128],
                     qkst[:, ln_, ch * 128:(ch + 1) * 128], ident[:, :],
                     is_transpose=True, start=True, stop=True,
                     w=p.bank_w(bk) + [kfm_done], i=[(sPE, 1)])
                p.on("vector", "tensor_copy", ktst[:, ks, :],
                     psum[:, bk, 0:128], w=[(sPE, sPE.n)], i=[(sDV, 1)])
                p.bank_done(bk, sDV)
                ktv = (sDV, sDV.n)
                for h2 in range(2):
                    h = 2 * m + h2
                    sbk = 4 + h // 8
                    start = (h % 8) == 0 and ch == 0
                    stop = ((h % 8) == 7 or h == H - 1) and ch == CHg - 1
                    wts = [ktv]
                    if start:
                        wts += p.bank_w(sbk)
                    p.on("tensor", "matmul",
                         psum[0:64, sbk, (h % 8) * 64:(h % 8) * 64 + 64],
                         ktst[:, ks, h2 * 64:h2 * 64 + 64],
                         Av[:, ch, h * 64:h * 64 + 64],
                         start=start, stop=stop, w=wts,
                         i=[(sPE, 1)])
        p.bar(1)
        sdone = (sPE, sPE.n)
        for sb_i in range(c.NSB):
            n = min(8, H - sb_i * 8) * 64
            p.on("vector", "tensor_copy", Ssum[:, sb_i * 512:sb_i * 512 + n],
                 psum[0:64, 4 + sb_i, 0:n], w=[sdone], i=[(sDV, 1)])
            p.bank_done(4 + sb_i, sDV)
        p.dma("gpsimd", io["cc_in"][:, :], Ssum[:, :], dCC, w=[(sDV, sDV.n)])
        p.wait_only("gpsimd", [(dCC, 16)])
        p.on("gpsimd", "collective_compute", "AllGather", OP.bypass,
             replica_groups=[[0, 1, 2, 3], [4, 5, 6, 7]],
             ins=[io["cc_in"]], outs=[io["cc_out"]], i=[(sCC, 1)])
        for m in range(KC):
            if m + 1 < KC:
                wst.request(io["wqs"][m + 1, :, :, :])
            s_, rdy = wst.take()
            ln_ = m % 2
            for tt in range(NT):
                bk = tt % 4
                wts = p.bank_w(bk) + ([rdy] if tt == 0 else [])
                tsl = slice(tt * TT, (tt + 1) * TT)
                for k in range(KC):
                    if not c.split:
                        st_q = wsl[:, s_, k, :]
                        mv_q = Bt[:, k, tsl]
                    else:
                        st_q = (wslb[:, s_, 0, k, :], wslb[:, s_, 1, k, :])
                        mv_q = (Bh[:, k, tsl], Bl[:, k, tsl])
                    mm3(psum[:, bk, :TT], st_q, mv_q,
                        start=(k == 0), stop=(k == KC - 1),
                        w=wts if k == 0 else [],
                        i=[(sPE, 1)] if k == KC - 1 else ())
                p.on("vector", "tensor_copy",
                     qkst[:, ln_, tt * TT:(tt + 1) * TT],
                     psum[:, bk, :TT],
                     w=[(sPE, sPE.n)] + ([lq.write_wait(ln_)] if tt == 0 else []),
                     i=[(sDV, 1)])
                p.bank_done(bk, sDV)
            wst.done(s_, sPE)
            lq.spill(ln_, io["scr_q"][m, :, :], qkst[:, ln_, :],
                     w=[(sDV, sDV.n)])
        if c.debug:
            p.dma("gpsimd", io["d_ssum"][:, :], Ssum[:, :], dDBG,
                  w=[(sDV, sDV.n)])
            p.dbg_guard = (dDBG, dDBG.n)

    p.barrier(exclude=(sCC,))
    with nc.sbuf_tensor("Mstart", [64, HHD], F32) as Mstart, \
            nc.sbuf_tensor("s_tril", [128, 128], F32) as tril, \
            nc.sbuf_tensor("s_pmask", [64, 4], F32) as pmask, \
            nc.sbuf_tensor("m2r", [64, 64], F32) as m2r:
        dTR = p.sem(stack, "dtr")
        p.dma("sync", tril[:, :], io["tril"][:, :], dTR,
              w=[(sPE, sPE.n), (sDV, sDV.n), (sAC, sAC.n)])
        p.dma("sync", pmask[:, :], io["pmask"][:, :], dTR)
        tri_r = (dTR, 32)
        p.on("vector", "memset", Mstart[:, :], 0.0, i=[(sDV, 1)])
        p.wait_only("gpsimd", [(sCC, sCC.n)])
        with nc.sbuf_tensor("Sj", [64, 2, HHD], F32) as Sj:
            lj = SpillLane(p, "gpsimd", stack, "lj", 2)
            for j in range(4):
                lj.spill(j % 2, Sj[:, j % 2, :], io["cc_out"][j, :, :],
                         w=[(sDV, sDV.n) if j >= 2 else None])
                p.on("vector", "scalar_tensor_tensor", Mstart[:, :],
                     Sj[:, j % 2, :], pmask[:, j:j + 1], Mstart[:, :],
                     OP.mult, OP.add, w=[lj.total(j % 2), tri_r],
                     i=[(sDV, 1)])
        ms_done = (sDV, sDV.n)
        if c.debug:
            p.dma("gpsimd", io["d_mstart"][:, :], Mstart[:, :], dDBG,
                  w=[ms_done])
            p.dbg_guard = (dDBG, dDBG.n)
        stack2 = stack.enter_context(contextlib.ExitStack())
        rlq = stack2.enter_context(nc.sbuf_tensor("rlq", [64, 4, T], F32))
        ktal = stack2.enter_context(nc.sbuf_tensor("ktal", [128, 2, 64], F32))
        yst = stack2.enter_context(nc.sbuf_tensor("yst", [64, 1, T], F32))
        msk = stack2.enter_context(nc.sbuf_tensor("msk", [128, CH, 128], F32))
        rst = TileStream(p, "scalar", [rlq[:, s, :] for s in range(4)],
                         stack, "trl")
        spq = [lq.total(0), lq.total(1)]
        ly = SpillLane(p, "sync", stack, "ly", 1)
        sj_free = [(sDV, sDV.n)]
        rst.request(io["scr_q"][0, 0:64, :], extra_w=spq + sj_free)
        rst.request(io["scr_kfm"][0, 0:64, :], extra_w=spq + sj_free)
        if H > 1:
            rst.request(io["scr_q"][0, 64:128, :], extra_w=spq + sj_free)
            rst.request(io["scr_kfm"][0, 64:128, :], extra_w=spq + sj_free)
        kt_guard = [None, None]
        for h in range(H):
            p.bar(2)
            qs, qr = rst.take()
            ks_, kr = rst.take()
            qt = rlq[:, qs, :]
            kt = rlq[:, ks_, :]
            for ch in range(CH):
                sbk = (ch % 2)
                p.on("tensor", "matmul", psum[0:128, sbk, 0:128],
                     kt[0:64, ch * 128:(ch + 1) * 128],
                     qt[0:64, ch * 128:(ch + 1) * 128],
                     start=True, stop=True,
                     w=p.bank_w(sbk) + ([qr, kr] if ch == 0 else []),
                     i=[(sPE, 1)])
                p.on("vector", "tensor_tensor", msk[:, ch, :],
                     psum[0:128, sbk, 0:128], tril[:, :], OP.mult,
                     w=[(sPE, sPE.n), tri_r], i=[(sDV, 1)])
                p.bank_done(sbk, sDV)
            sc_done = (sDV, sDV.n)
            p.on("vector", "tensor_copy", m2r[:, :],
                 Mstart[0:64, h * 64:h * 64 + 64],
                 w=[ms_done, (sPE, sPE.n)], i=[(sDV, 1)])
            for ch in range(CH):
                ybk = 2 + (ch % 2)
                m2v = (sDV, sDV.n)
                if ch < CH - 1:
                    ks2 = ch % 2
                    bk = 6 + ks2
                    p.on("tensor", "matmul", psum[0:128, bk, 0:64],
                         kt[0:64, ch * 128:(ch + 1) * 128], ident[0:64, 0:64],
                         is_transpose=True, start=True, stop=True,
                         w=p.bank_w(bk), i=[(sPE, 1)])
                    p.on("vector", "tensor_copy", ktal[:, ks2, :],
                         psum[0:128, bk, 0:64],
                         w=[(sPE, sPE.n), kt_guard[ks2]], i=[(sDV, 1)])
                    p.bank_done(bk, sDV)
                p.on("tensor", "matmul", psum[0:64, ybk, 0:128], m2r[:, :],
                     qt[0:64, ch * 128:(ch + 1) * 128], start=True, stop=False,
                     w=p.bank_w(ybk) + [m2v])
                p.on("tensor", "matmul", psum[0:64, ybk, 0:128],
                     Av[:, ch, h * 64:h * 64 + 64], msk[:, ch, :],
                     start=False, stop=True, w=[sc_done], i=[(sPE, 1)])
                ydst = (Bt[0:64, h // 2, ch * 128:(ch + 1) * 128]
                        if h % 2 == 0 else
                        yst[:, 0, ch * 128:(ch + 1) * 128])
                p.on("scalar", "copy", ydst, psum[0:64, ybk, 0:128],
                     w=[(sPE, sPE.n)]
                     + ([ly.write_wait(0)] if (ch == 0 and h % 2 == 1) else []),
                     i=[(sAC, 1)])
                p.bank_done(ybk, sAC)
                if ch < CH - 1:
                    pbk = 4
                    p.on("tensor", "matmul", psum[0:64, pbk, 0:64],
                         ktal[:, ch % 2, :], Av[:, ch, h * 64:h * 64 + 64],
                         start=True, stop=True,
                         w=p.bank_w(pbk) + [(sDV, sDV.n)], i=[(sPE, 1)])
                    kt_guard[ch % 2] = (sPE, sPE.n)
                    p.on("vector", "tensor_tensor", m2r[:, :], m2r[:, :],
                         psum[0:64, pbk, 0:64], OP.add,
                         w=[(sPE, sPE.n)], i=[(sDV, 1)])
                    p.bank_done(pbk, sDV)
            y_done = (sAC, sAC.n)
            if h % 2 == 1:
                ly.spill(0, Bt[64:128, h // 2, :], yst[:, 0, :], w=[y_done])
            rst.done(qs, sPE)
            rst.done(ks_, sPE)
            nh = h + 2
            if nh < H:
                rst.request(io["scr_q"][nh // 2, (nh % 2) * 64:(nh % 2) * 64 + 64, :],
                            extra_w=spq)
                rst.request(io["scr_kfm"][nh // 2, (nh % 2) * 64:(nh % 2) * 64 + 64, :],
                            extra_w=spq)
        y_all = [(sDV, sDV.n), (sAC, sAC.n), ly.total(0)]
        dump("d_y", Bt[:, :, :], y_all)
        stack2.close()

    p.barrier()
    _raq = ["scalar", "sync", "gpsimd"]
    dRAf = [p.sem(stack, f"dRA{f}") for f in range(KC)]
    for f in range(KC):
        p.dma(_raq[f % 3], A[:, f, :], io["scr_ain"][:, f, :], dRAf[f],
              w=[(sPE, sPE.n), la.total(0), la.total(1)])
    ain_back = None
    ain_back_f = [(dRAf[f], 16) for f in range(KC)]
    with nc.sbuf_tensor("wsl2", [128, 2, KC, 128], F32) as wsl2:
        wst2 = TileStream(p, "sync", [wsl2[:, s, :, :] for s in range(2)],
                          stack, "two")
        wst2.request(io["wos"][0, :, :, :])
        for oc in range(KC):
            if oc + 1 < KC:
                wst2.request(io["wos"][oc + 1, :, :, :])
            s_, rdy = wst2.take()
            for tt in range(NT):
                bk = tt % 4
                wts = p.bank_w(bk) + ([rdy] + y_all if tt == 0 else [])
                for k in range(KC):
                    p.on("tensor", "matmul", psum[:, bk, :TT],
                         wsl2[:, s_, k, :], Bt[:, k, tt * TT:(tt + 1) * TT],
                         start=(k == 0), stop=(k == KC - 1),
                         w=wts if k == 0 else [],
                         i=[(sPE, 1)] if k == KC - 1 else ())
                p.on("vector", "tensor_tensor",
                     A[:, oc, tt * TT:(tt + 1) * TT],
                     A[:, oc, tt * TT:(tt + 1) * TT], psum[:, bk, :TT],
                     OP.add, w=[(sPE, sPE.n), ain_back_f[oc]], i=[(sDV, 1)])
                p.bank_done(bk, sDV)
            wst2.done(s_, sPE)
    x2p_done = (sDV, sDV.n)
    dump("d_x2p", A[:, :, :])

    p.barrier()
    dRXf = [p.sem(stack, f"dRX{f}") for f in range(KC)]
    for f in range(KC):
        p.dma(_raq[f % 3], Bt[:, f, :], io["scr_x1"][:, f, :], dRXf[f],
              w=[(sPE, sPE.n), (sDV, sDV.n), (dX1, dX1.n)])
    for f in range(KC):
        p.on("vector", "tensor_tensor", Bt[:, f, :], Bt[:, f, :],
             A[:, f, :], OP.add,
             w=[(dRXf[f], 16)] + ([x2p_done] if f == 0 else []),
             i=[(sDV, 1)])
    p.dma("scalar", io["scr_acc"][:, :, :], Bt[:, :, :], dACC,
          w=[(sDV, sDV.n)])
    acc_spill = (dACC, 16)
    ln_stats(lambda f: A[:, f, :], x2p_done)
    hst = (sDV, sDV.n)
    for f in range(KC):
        p.on("vector", "tensor_tensor", sqst[:, :], A[:, f, :],
             U16a[:, :], OP.mult,
             w=[hst, acc_spill] if f == 0 else [], i=[(sDV, 1)])
        p.on("vector", "tensor_tensor", sqst[:, :], sqst[:, :],
             U16b[:, :], OP.add, i=[(sDV, 1)])
        if not c.split:
            p.on("vector", "tensor_scalar", Bt[:, f, :], sqst[:, :],
                 lnv[:, 2 * KC + f:2 * KC + f + 1],
                 lnv[:, 3 * KC + f:3 * KC + f + 1], OP.mult, OP.add,
                 i=[(sDV, 1)])
        else:
            p.on("vector", "tensor_scalar", sqst[:, :], sqst[:, :],
                 lnv[:, 2 * KC + f:2 * KC + f + 1],
                 lnv[:, 3 * KC + f:3 * KC + f + 1], OP.mult, OP.add,
                 i=[(sDV, 1)])
            split2(Bh[:, f, :], Bl[:, f, :], sqst[:, :])
    lvl_ready = (sDV, sDV.n)
    h3 = A if c.LEVELS % 2 == 1 else Bt
    res2 = Bt if c.LEVELS % 2 == 1 else A
    dRCf = [p.sem(stack, f"dRC{f}") for f in range(KC)]
    _rcq = ["scalar", "gpsimd"]

    p.barrier()
    CMSHC = c.CMSHC
    with contextlib.ExitStack() as cstk:
        if not c.split:
            hid = cstk.enter_context(
                nc.sbuf_tensor("hid", [128, CMSHC, TT], F32))
            if T >= 2 * D and T >= 2 * KC * 128:
                cwsl = U16b[:, 0:2 * KC * 128].rearrange(
                    "p (s k q) -> p s k q", s=2, q=128)
                cw2l = U16a[:, 0:2 * D].rearrange("p (s d) -> p s d", d=D)
            else:
                cwsl = cstk.enter_context(
                    nc.sbuf_tensor("cwsl", [128, 2, KC, 128], F32))[:, :, :, :]
                cw2l = cstk.enter_context(
                    nc.sbuf_tensor("cw2l", [128, 2, D], F32))[:, :, :]
            c1st = TileStream(p, "sync", [cwsl[:, s, :, :] for s in range(2)],
                              stack, "tc1")
            c2st = TileStream(p, "sync", [cw2l[:, s, :] for s in range(2)],
                              stack, "tc2")
        else:
            hid_h = cstk.enter_context(
                nc.sbuf_tensor("hid_h", [128, CMSHC, TT], BF16))
            hid_l = cstk.enter_context(
                nc.sbuf_tensor("hid_l", [128, CMSHC, TT], BF16))
            if 2 * T >= 2 * 2 * KC * 128 and 2 * T >= 4 * D:
                c1b = U16bb[:, 0:4 * KC * 128].rearrange(
                    "p (s h k q) -> p s h k q", s=2, h=2, q=128)
                c2b = U16ab[:, 0:4 * D].rearrange(
                    "p (s h d) -> p s h d", s=2, d=D)
            else:
                c1b = cstk.enter_context(
                    nc.sbuf_tensor("c1b", [128, 2, 2, KC, 128], BF16))[:, :, :, :, :]
                c2b = cstk.enter_context(
                    nc.sbuf_tensor("c2b", [128, 2, 2, D], BF16))[:, :, :, :]
            c1st = SplitTileStream(p, pump,
                                   [(c1b[:, s, 0, :, :], c1b[:, s, 1, :, :])
                                    for s in range(2)])
            c2st = SplitTileStream(p, pump,
                                   [(c2b[:, s, 0, :], c2b[:, s, 1, :])
                                    for s in range(2)])
        for _st in (c1st, c2st):
            _st.guard[0] = lvl_ready
            _st.guard[1] = lvl_ready
        for lvl in range(c.LEVELS):
            p.bar(2)
            src = Bt if lvl % 2 == 0 else A
            dst = A if lvl % 2 == 0 else Bt
            src_ready = lvl_ready
            for tg in range(NT):
                sl = slice(tg * TT, (tg + 1) * TT)
                if lvl == 0 and tg == 0:
                    c1st.request(io["cw1s"][lvl, 0, :, :, :])
                for hc in range(CMSHC):
                    if hc + 1 < CMSHC:
                        c1st.request(io["cw1s"][lvl, hc + 1, :, :, :])
                    else:
                        c2st.request(io["cw2s"][lvl, 0, :, :])
                        if tg + 1 < NT:
                            c1st.request(io["cw1s"][lvl, 0, :, :, :])
                        elif lvl + 1 < c.LEVELS:
                            c1st.request(io["cw1s"][lvl + 1, 0, :, :, :])
                    s_, rdy = c1st.take()
                    bk = hc % 2
                    wts = p.bank_w(bk) + [rdy] + \
                        ([src_ready] if hc == 0 else [])
                    for k in range(KC):
                        if not c.split:
                            st_c = cwsl[:, s_, k, :]
                            mv_c = src[:, k, sl]
                        else:
                            st_c = (c1b[:, s_, 0, k, :], c1b[:, s_, 1, k, :])
                            sh_, sl2_ = hilo[id(src)]
                            mv_c = (sh_[:, k, sl], sl2_[:, k, sl])
                        mm3(psum[:, bk, :TT], st_c, mv_c,
                            start=(k == 0), stop=(k == KC - 1),
                            w=wts if k == 0 else [],
                            i=[(sPE, 1)] if k == KC - 1 else ())
                    cbi = lvl * CMSHC + hc
                    if not c.split:
                        p.on("scalar", "activation", hid[:, hc, :TT],
                             psum[:, bk, :TT], AF.Gelu_apprx_tanh,
                             bias=cb1s[:, cbi:cbi + 1], w=[(sPE, sPE.n)],
                             i=[(sAC, 1)])
                        p.bank_done(bk, sAC)
                    else:
                        p.on("scalar", "activation", sqst[:, 0:TT],
                             psum[:, bk, :TT], AF.Gelu_apprx_tanh,
                             bias=cb1s[:, cbi:cbi + 1],
                             w=[(sPE, sPE.n), (sDV, sDV.n)], i=[(sAC, 1)])
                        split2(hid_h[:, hc, :TT], hid_l[:, hc, :TT],
                               sqst[:, 0:TT], w=[(sAC, sAC.n)])
                        p.bank_done(bk, sDV)
                    c1st.done(s_, sPE)
                hid_done = (sDV, sDV.n) if c.split else (sAC, sAC.n)
                if lvl == c.LEVELS - 1 and tg == NT - 1:
                    for f in range(KC):
                        p.dma(_rcq[f % 2], res2[:, f, :],
                              io["scr_acc"][:, f, :], dRCf[f],
                              w=[(sPE, sPE.n), (sDV, sDV.n), acc_spill])
                for hc in range(CMSHC):
                    if hc + 1 < CMSHC:
                        c2st.request(io["cw2s"][lvl, hc + 1, :, :])
                    s_, rdy = c2st.take()
                    for oc in range(KC):
                        wts = [rdy] if oc == 0 else []
                        if hc == 0:
                            wts = wts + p.bank_w(oc) + \
                                ([hid_done] if oc == 0 else [])
                        osl = slice(oc * 128, (oc + 1) * 128)
                        if not c.split:
                            st_c = cw2l[:, s_, osl]
                            mv_c = hid[:, hc, :TT]
                        else:
                            st_c = (c2b[:, s_, 0, osl], c2b[:, s_, 1, osl])
                            mv_c = (hid_h[:, hc, :TT], hid_l[:, hc, :TT])
                        mm3(psum[:, oc, :TT], st_c, mv_c, start=(hc == 0),
                            stop=(hc == CMSHC - 1), w=wts,
                            i=[(sPE, 1)] if oc == KC - 1 else ())
                    c2st.done(s_, sPE)
                od = (sPE, sPE.n)
                for oc in range(KC):
                    cbo = lvl * KC + oc
                    if not c.split or lvl == c.LEVELS - 1:
                        p.on("vector", "tensor_scalar_add", dst[:, oc, sl],
                             psum[:, oc, :TT], cb2s[:, cbo:cbo + 1],
                             w=[od], i=[(sDV, 1)])
                    else:
                        p.on("vector", "tensor_scalar_add", sqst[:, 0:TT],
                             psum[:, oc, :TT], cb2s[:, cbo:cbo + 1],
                             w=[od], i=[(sDV, 1)])
                        dh_, dl_ = hilo[id(dst)]
                        split2(dh_[:, oc, sl], dl_[:, oc, sl], sqst[:, 0:TT])
                    p.bank_done(oc, sDV)
            lvl_ready = (sDV, sDV.n)
    for f in range(KC):
        p.on("vector", "tensor_tensor", res2[:, f, :], res2[:, f, :],
             h3[:, f, :], OP.add,
             w=[(dRCf[f], 16)] + ([lvl_ready] if f == 0 else []),
             i=[(sDV, 1)])
    dump("d_x2", res2[:, :, :])

    p.barrier()
    outv = io["out"].rearrange("(k p) t -> p k t", p=128)
    _oq = ["scalar", "sync", "gpsimd"]

    def out_cb(oc, rdy):
        p.dma(_oq[oc % 3], outv[:, oc, :], res2[:, oc, :], dOUT, w=[rdy])
    emit_ffn(res2, h3, (sDV, sDV.n), out_cb=out_cb)
    p.wait_only("scalar", [(dOUT, 16 * KC)])
    if c.debug:
        p.wait_only("gpsimd", [(dDBG, dDBG.n)])



def kernel(**inputs):
    cfg = CFG(split=True)
    in_maps = stage_inputs(inputs, cfg)
    nc = build(cfg)
    res = run_bass_kernel_spmd(nc, in_maps, core_ids=list(range(8)))
    out = np.empty((2, 8192, 1024), dtype=np.float32)
    for core in range(8):
        b, seg = core // 4, core % 4
        out[b, seg * cfg.T:(seg + 1) * cfg.T, :] = \
            np.asarray(res.results[core]["out"]).T
    return out
